# version 18
# baseline (speedup 1.0000x reference)
import math
import contextlib
import numpy as np
import ml_dtypes
import concourse.bass as bass
import concourse.mybir as mybir
import concourse.bass_utils as _bass_utils
from concourse.bass_utils import run_bass_kernel_spmd


F32 = mybir.dt.float32
F32R = mybir.dt.float32r
BF16 = mybir.dt.bfloat16


def R32(ap):
    return ap.bitcast(F32R)

U32 = mybir.dt.uint32
AF = mybir.ActivationFunctionType
ALU = mybir.AluOpType
AX = mybir.AxisListType

D = 1024
T = 2048
NFC = 8
NTC = 4
TC = 512
NST = 16
LAMBDA_INIT = float(0.8 - 0.6 * np.exp(-0.3 * 12))
SC0 = 1.0 / math.sqrt(32.0)
SC1 = 0.125
EPS = 1e-5

CFG = {
    "layers": (0, 1),
    "debug": (),
    "n_cores": 8,
    "l0_upto": "ffn",
    "l1_upto": "moe",
}


def _get_slopes(n):
    def pow2(n):
        start = 2 ** (-(2 ** (-(math.log2(n) - 3))))
        return [start * start ** i for i in range(n)]
    if math.log2(n).is_integer():
        return pow2(n)
    c = 2 ** math.floor(math.log2(n))
    return pow2(c) + _get_slopes(2 * c)[0::2][:n - c]



class Buf:
    __slots__ = ("wtok", "rtoks", "name")

    def __init__(self, name=""):
        self.wtok = None
        self.rtoks = []
        self.name = name


KEYS = ("s", "g", "t", "v", "c")


class Prog:
    def __init__(self):
        self.items = {k: [] for k in KEYS}
        self.count = {k: 0 for k in KEYS}
        self.seen = {k: {} for k in KEYS}
        self.nblocks = 0

    def op(self, k, fn, reads=(), writes=()):
        deps = []
        for b in reads:
            if b.wtok is not None:
                deps.append(b.wtok)
        for b in writes:
            if b.wtok is not None:
                deps.append(b.wtok)
            deps.extend(b.rtoks)
        it = self.items[k]
        seen = self.seen[k]
        for dk, dv in deps:
            if seen.get(dk, 0) >= dv:
                continue
            it.append(("w", dk, dv))
            seen[dk] = dv
        amt = 16 if k in ("s", "g") else 1
        self.count[k] += amt
        tok = (k, self.count[k])
        it.append(("o", fn, amt))
        for b in writes:
            b.wtok = tok
            b.rtoks = []
        for b in reads:
            if tok not in b.rtoks:
                b.rtoks.append(tok)
        return tok

    def barrier(self):
        for k in KEYS:
            for j in KEYS:
                v = self.count[j]
                if v and self.seen[k].get(j, 0) < v:
                    self.items[k].append(("w", j, v))
                    self.seen[k][j] = v

    def flush(self, nc, sems):
        items = self.items
        self.items = {k: [] for k in KEYS}
        live = [k for k in KEYS if items[k]]
        if not live:
            return
        self.nblocks += 1

        def mkbody(k):
            def body(eng):
                for it in items[k]:
                    if it[0] == "w":
                        eng.wait_ge(sems[it[1]], it[2])
                    else:
                        instr = it[1](eng)
                        instr.then_inc(sems[k], it[2])
            return body

        with nc.Block(f"ph{self.nblocks}") as block:
            if "s" in live:
                block.sync(mkbody("s"))
            if "g" in live:
                block.gpsimd(mkbody("g"))
            if "t" in live:
                block.tensor(mkbody("t"))
            if "v" in live:
                block.vector(mkbody("v"))
            if "c" in live:
                block.scalar(mkbody("c"))


class SB:
    __slots__ = ("t", "b")

    def __init__(self, t, b):
        self.t = t
        self.b = b

    def __getitem__(self, idx):
        return self.t.ap()[idx]

    @property
    def ap(self):
        return self.t.ap()


class Builder:
    def __init__(self, cfg):
        self.cfg = cfg
        self.nc = bass.Bass("TRN2", target_bir_lowering=False)
        self.P = Prog()
        self.dram = {}
        self.debug_outs = []
        self.bufs = {}
        self.consts = {}

    def din(self, name, shape, dt=F32):
        ap = self.nc.dram_tensor(name, list(shape), dt, kind="ExternalInput").ap()
        self.dram[name] = ap
        self.bufs[name] = Buf(name)
        return ap

    def dout(self, name, shape, dt=F32):
        ap = self.nc.dram_tensor(name, list(shape), dt, kind="ExternalOutput").ap()
        self.dram[name] = ap
        self.bufs[name] = Buf(name)
        return ap

    def dscratch(self, name, shape, dt):
        kind = "ExternalOutput" if name in self.cfg["debug"] else "Internal"
        if kind == "ExternalOutput":
            self.debug_outs.append(name)
        ap = self.nc.dram_tensor(name, list(shape), dt, kind=kind).ap()
        self.dram[name] = ap
        self.bufs[name] = Buf(name)
        return ap

    _uid = 0

    def _uniq(self, name):
        Builder._uid += 1
        return f"{name}__{Builder._uid}"

    def sbuf(self, es, name, shape, dt):
        t = es.enter_context(self.nc.sbuf_tensor(self._uniq(name), list(shape), dt))
        return SB(t, Buf(name))

    def psum(self, es, name, shape, dt=F32):
        t = es.enter_context(self.nc.psum_tensor(self._uniq(name), list(shape), dt))
        return SB(t, Buf(name))

    def tap(self, name, src_sb, shape, dt=F32):
        if name not in self.cfg["debug"]:
            return
        nm = "dbg_" + name
        ap = self.dout(nm, shape, dt)
        self.debug_outs.append(nm)
        self.P.op("s", lambda e, ap=ap, s=src_sb: e.dma_start(out=ap[:, :], in_=s.ap[:, :]),
                  reads=[src_sb.b], writes=[self.bufs[nm]])



def build_program(cfg):
    B = Builder(cfg)
    nc, P = B.nc, B.P

    B.din("xT", [D, T])
    B.din("ID128", [128, 128])
    B.dout("out", [D, T])
    if 0 in cfg["layers"]:
        B.din("L0WqT", [D, D], BF16); B.din("L0WkT", [D, D], BF16); B.din("L0WvT", [D, D], BF16); B.din("L0WoT", [D, D], BF16)
        B.din("cnw1", [D, D], BF16); B.din("cnw2", [D, 512], BF16); B.din("cnw3", [512, 256], BF16); B.din("cnw4", [256, 1], BF16)
        B.din("gcw1", [D, 512], BF16); B.din("gcw2", [512, 256], BF16); B.din("gcw3", [256, 1], BF16)
        B.din("cnb1", [128, 8]); B.din("cnb2", [128, 4]); B.din("cnb3", [128, 2]); B.din("cnb4", [1, 1])
        B.din("gcb1", [128, 4]); B.din("gcb2", [128, 2]); B.din("gcb3", [1, 1])
        B.din("lamq1", [1, 32]); B.din("lamk1", [1, 32]); B.din("lamq2", [1, 32]); B.din("lamk2", [1, 32])
        B.din("subln", [64, 1]); B.din("arange", [1, 1])
    if 1 in cfg["layers"]:
        B.din("L1Wq", [D, D], BF16); B.din("L1Wk", [D, D], BF16); B.din("L1Wv", [D, D], BF16); B.din("L1Wo", [D, D], BF16)
        B.din("bq", [128, 8]); B.din("bk", [128, 8]); B.din("bv", [1, D]); B.din("bo", [128, 8])
        B.din("sw1c", [4096, D], BF16); B.din("sw2c", [D, 4096], BF16)
        for e in range(8):
            B.din(f"ew1c{e}", [4096, D], BF16); B.din(f"ew2c{e}", [D, 4096], BF16)
        B.din("gwT", [D, 8])
        B.din("ALB", [128, 256])
        B.din("A8", [16, T])
        B.din("CM", [128, 4 * TC], BF16)
        B.din("IOTA8", [128, 8])
    B.dscratch("xn32s", [D, T], F32)
    B.dscratch("qTs", [D, T], BF16)
    B.dscratch("kTs", [D, T], BF16)
    if 1 in cfg["layers"]:
        B.dscratch("grows", [8, T], F32)
        B.dscratch("xn16s", [D, T], BF16)
        B.dscratch("x32s", [D, T], F32)
        B.dscratch("y32s", [D, T], F32)

    with contextlib.ExitStack() as es0:
        sems = {k: es0.enter_context(nc.semaphore("sem_" + k)) for k in KEYS}

        ONESC = B.sbuf(es0, "ONESC", [128, 1], F32)
        ONESR = B.sbuf(es0, "ONESR", [1, 128], F32)
        ONESR16 = B.sbuf(es0, "ONESR16", [1, 128], BF16)
        EPSA = B.sbuf(es0, "EPSA", [1, 1], F32)
        B.esX32 = contextlib.ExitStack()
        X32 = [B.sbuf(B.esX32, f"X32_{i}", [128, T], F32) for i in range(NFC)]
        B.x32_parked = False
        B.consts = {"X32": X32, "ONESC": ONESC, "ONESR": ONESR,
                    "ONESR16": ONESR16, "EPSA": EPSA}

        P.op("v", lambda e: e.memset(ONESC[:, :], 1.0), writes=[ONESC.b])
        P.op("v", lambda e: e.memset(ONESR[:, :], 1.0), writes=[ONESR.b])
        P.op("v", lambda e: e.memset(ONESR16[:, :], 1.0), writes=[ONESR16.b])
        P.op("v", lambda e: e.memset(EPSA[:, :], EPS), writes=[EPSA.b])
        for i in range(NFC):
            P.op("s", lambda e, i=i: e.dma_start(out=X32[i][:, :], in_=B.dram["xT"][i * 128:(i + 1) * 128, :]),
                 reads=[B.bufs["xT"]], writes=[X32[i].b])
        P.flush(nc, sems)

        if 0 in cfg["layers"]:
            build_layer0(B, sems)
        if 1 in cfg["layers"]:
            build_layer1(B, sems)

        if not B.x32_parked:
            for i in range(NFC):
                P.op("s", lambda e, i=i: e.dma_start(out=B.dram["out"][i * 128:(i + 1) * 128, :],
                                                     in_=X32[i][:, :]),
                     reads=[X32[i].b], writes=[B.bufs["out"]])
        else:
            with contextlib.ExitStack() as ef:
                FA = [B.sbuf(ef, f"FA{i}", [128, T], F32) for i in range(2)]
                FB = [B.sbuf(ef, f"FB{i}", [128, T], F32) for i in range(2)]
                for i in range(NFC):
                    fa, fb = FA[i % 2], FB[i % 2]
                    P.op("s", lambda e, i=i, fa=fa: e.dma_start(
                            out=fa[:, :], in_=B.dram["x32s"][i * 128:(i + 1) * 128, :]),
                         reads=[B.bufs["x32s"]], writes=[fa.b])
                    P.op("g", lambda e, i=i, fb=fb: e.dma_start(
                            out=fb[:, :], in_=B.dram["y32s"][i * 128:(i + 1) * 128, :]),
                         reads=[B.bufs["y32s"]], writes=[fb.b])
                    P.op("v", lambda e, fa=fa, fb=fb: e.tensor_tensor(fa[:, :], fa[:, :], fb[:, :], ALU.add),
                         reads=[fb.b, fa.b], writes=[fa.b])
                    P.op("s", lambda e, i=i, fa=fa: e.dma_start(
                            out=B.dram["out"][i * 128:(i + 1) * 128, :], in_=fa[:, :]),
                         reads=[fa.b], writes=[B.bufs["out"]])
                P.barrier()
        P.items["s"].append(("w", "s", P.count["s"]))
        P.flush(nc, sems)

    return B



def layernorm_rows(B, es_keep, es_scr, tag):
    P = B.P
    X32 = B.consts["X32"]
    ONESC = B.consts["ONESC"]
    EPSA = B.consts["EPSA"]
    M = B.sbuf(es_keep, f"MROW{tag}", [1, T], F32)
    R = B.sbuf(es_keep, f"RROW{tag}", [1, T], F32)
    SQ = [B.sbuf(es_scr, f"LNSQ{tag}_{i}", [128, TC], F32) for i in range(2)]
    PSm = B.psum(es_scr, f"PLNm{tag}", [1, TC], F32)
    PSq = B.psum(es_scr, f"PLNq{tag}", [1, TC], F32)
    for tc in range(NTC):
        sl = slice(tc * TC, (tc + 1) * TC)

        def mk_sum(e, sl=sl):
            last = None
            for i in range(NFC):
                last = e.matmul(PSm[0:1, :], ONESC[:, :], X32[i][:, sl],
                                start=(i == 0), stop=(i == NFC - 1))
            return last
        P.op("t", mk_sum, reads=[X32[i].b for i in range(NFC)] + [ONESC.b], writes=[PSm.b])
        for i in range(NFC):
            sq = SQ[i % 2]
            P.op("c", lambda e, i=i, sl=sl, sq=sq: e.activation(sq[:, :], X32[i][:, sl], AF.Square),
                 reads=[X32[i].b], writes=[sq.b])
            P.op("t", lambda e, i=i, sq=sq: e.matmul(PSq[0:1, :], ONESC[:, :], sq[:, :],
                                                     start=(i == 0), stop=(i == NFC - 1)),
                 reads=[sq.b, ONESC.b], writes=[PSq.b])
        P.op("v", lambda e, sl=sl: e.tensor_scalar(M[0:1, sl], PSm[0:1, :], 1.0 / D, None, ALU.mult),
             reads=[PSm.b], writes=[M.b])
        P.op("v", lambda e, sl=sl: e.tensor_tensor(R[0:1, sl], M[0:1, sl], M[0:1, sl], ALU.mult),
             reads=[M.b], writes=[R.b])
        P.op("v", lambda e, sl=sl: e.scalar_tensor_tensor(R[0:1, sl], PSq[0:1, :], 1.0 / D, R[0:1, sl],
                                                          ALU.mult, ALU.subtract),
             reads=[PSq.b, R.b], writes=[R.b])
        P.op("c", lambda e, sl=sl: e.activation(R[0:1, sl], R[0:1, sl], AF.Ln, bias=EPSA[0:1, :]),
             reads=[R.b, EPSA.b], writes=[R.b])
        P.op("c", lambda e, sl=sl: e.activation(R[0:1, sl], R[0:1, sl], AF.Exp, scale=-0.5),
             reads=[R.b], writes=[R.b])
    return M, R


def ln_apply_bf16(B, es, M, R, XN16, park_name=None, gate_cb=None):
    P = B.P
    X32 = B.consts["X32"]
    ONESR = B.consts["ONESR"]
    PB = B.psum(es, "PLNB", [128, 1024], F32)
    XA = [B.sbuf(es, f"LNXA{i}", [128, TC], F32) for i in range(NFC if gate_cb else 2)]
    HBa = B.sbuf(es, "HBa", [1, TC], BF16)
    LBa = B.sbuf(es, "LBa", [1, TC], BF16)
    DFa = B.sbuf(es, "DFa", [1, TC], F32)
    for tc in range(NTC):
        sl = slice(tc * TC, (tc + 1) * TC)
        bcast_hl(B, HBa, LBa, DFa, M[0:1, sl], PB[:, 0:TC], 128, [M.b], PB.b)
        bcast_hl(B, HBa, LBa, DFa, R[0:1, sl], PB[:, TC:2 * TC], 128, [R.b], PB.b)
        for i in range(NFC):
            xa = XA[i] if gate_cb else XA[i % 2]
            P.op("v", lambda e, i=i, sl=sl, xa=xa: e.tensor_tensor(xa[:, :], X32[i][:, sl], PB[:, 0:TC], ALU.subtract),
                 reads=[X32[i].b, PB.b], writes=[xa.b])
            P.op("v", lambda e, i=i, sl=sl, xa=xa: e.tensor_tensor(xa[:, :], xa[:, :], PB[:, TC:2 * TC], ALU.mult),
                 reads=[xa.b, PB.b], writes=[xa.b])
            P.op("v", lambda e, i=i, sl=sl, xa=xa: e.tensor_copy(XN16[i][:, sl], xa[:, :]),
                 reads=[xa.b], writes=[XN16[i].b])
            if park_name is not None:
                P.op("s", lambda e, i=i, sl=sl, xa=xa: e.dma_start(
                        out=B.dram[park_name][i * 128:(i + 1) * 128, sl], in_=xa[:, :]),
                     reads=[xa.b], writes=[B.bufs[park_name]])
        if gate_cb is not None:
            gate_cb(tc, XA)


def bcast_hl(B, HB, LB, DF, src_ap, out_ap, m, reads, outbuf, scale_ap=None, scale_reads=()):
    P = B.P
    ONESR16 = B.consts["ONESR16"]
    n = src_ap.shape[-1]
    P.op("v", lambda e: e.tensor_copy(DF[0:1, 0:n], src_ap), reads=reads, writes=[DF.b])
    if scale_ap is not None:
        P.op("v", lambda e: e.tensor_scalar(DF[0:1, 0:n], DF[0:1, 0:n], scale_ap, None, ALU.mult),
             reads=[DF.b] + list(scale_reads), writes=[DF.b])
    P.op("v", lambda e: e.tensor_copy(HB[0:1, 0:n], DF[0:1, 0:n]), reads=[DF.b], writes=[HB.b])
    P.op("v", lambda e: e.tensor_tensor(DF[0:1, 0:n], DF[0:1, 0:n], HB[0:1, 0:n], ALU.subtract),
         reads=[DF.b, HB.b], writes=[DF.b])
    P.op("v", lambda e: e.tensor_copy(LB[0:1, 0:n], DF[0:1, 0:n]), reads=[DF.b], writes=[LB.b])

    def mk(e):
        e.matmul(out_ap, ONESR16[0:1, 0:m], HB[0:1, 0:n], start=True, stop=False,
                 skip_group_check=True)
        return e.matmul(out_ap, ONESR16[0:1, 0:m], LB[0:1, 0:n], start=False, stop=True,
                        skip_group_check=True)
    P.op("t", mk, reads=[HB.b, LB.b, ONESR16.b], writes=[outbuf])


def load_w_bf16(B, es, src_name, rows, cols, tag):
    P = B.P
    n = rows // 128
    tiles = [B.sbuf(es, f"W{tag}_{i}", [128, cols], BF16) for i in range(n)]
    for i in range(n):
        P.op("g", lambda e, i=i: e.dma_start(out=tiles[i][:, :],
                                             in_=B.dram[src_name][i * 128:(i + 1) * 128, :]),
             reads=[B.bufs[src_name]], writes=[tiles[i].b])
    return tiles


def proj_qkv(B, es, specs, XN16, V=None, vspec=None):
    P = B.P
    PQ2 = [B.psum(es, f"PQq{i}", [128, 2048], F32) for i in range(2)]
    QS = [B.sbuf(es, f"QSq{i}", [128, T], BF16) for i in range(1)]
    ALLW = {wname: load_w_bf16(B, es, wname, D, D, "pj" + wname)
            for wname, _, _ in specs}
    if vspec is not None:
        ALLW[vspec[0]] = load_w_bf16(B, es, vspec[0], D, D, "pjv")
    for wname, store_name, bias in specs:
        if True:
            W = ALLW[wname]
            for fo in range(NFC):
                PQ = PQ2[fo % 2]

                def mk(e, fo=fo, PQ=PQ, W=W):
                    last = None
                    for ec in range(NFC):
                        for tc in range(NTC):
                            last = e.matmul(PQ[:, tc * TC:(tc + 1) * TC],
                                            W[ec][:, fo * 128:(fo + 1) * 128],
                                            XN16[ec][:, tc * TC:(tc + 1) * TC],
                                            start=(ec == 0), stop=(ec == NFC - 1),
                                            skip_group_check=True)
                    return last
                P.op("t", mk, reads=[W[ec].b for ec in range(NFC)] + [XN16[ec].b for ec in range(NFC)],
                     writes=[PQ.b])
                qs = QS[0]
                if bias is not None:
                    P.op("c", lambda e, qs=qs, fo=fo, PQ=PQ, bias=bias: e.add(qs[:, :], PQ[:, :],
                                                                              bias[:, fo:fo + 1]),
                         reads=[PQ.b, bias.b], writes=[qs.b])
                else:
                    P.op("v", lambda e, qs=qs, PQ=PQ: e.tensor_copy(qs[:, :], PQ[:, :]),
                         reads=[PQ.b], writes=[qs.b])
                P.op("s", lambda e, fo=fo, qs=qs, store_name=store_name: e.dma_start(
                        out=B.dram[store_name][fo * 128:(fo + 1) * 128, :], in_=qs[:, :]),
                     reads=[qs.b], writes=[B.bufs[store_name]])
    if vspec is not None:
        wname, bvb = vspec
        with contextlib.ExitStack() as ew:
            W = ALLW[wname]
            VS = [B.sbuf(ew, f"VSq{i}", [128, 1024], F32) for i in range(1)]
            for st in range(NST):
                PV = PQ2[st % 2]

                def mkv(e, st=st, PV=PV, W=W):
                    last = None
                    for ec in range(NFC):
                        for half in range(2):
                            last = e.matmul(PV[:, half * 512:(half + 1) * 512],
                                            XN16[ec][:, st * 128:(st + 1) * 128],
                                            W[ec][:, half * 512:(half + 1) * 512],
                                            start=(ec == 0), stop=(ec == NFC - 1),
                                            skip_group_check=True)
                    return last
                P.op("t", mkv, reads=[W[ec].b for ec in range(NFC)] + [XN16[ec].b for ec in range(NFC)],
                     writes=[PV.b])
                vs = VS[0]
                if bvb is not None:
                    P.op("v", lambda e, vs=vs, PV=PV: e.tensor_tensor(vs[:, :], PV[:, 0:1024], bvb[:, :], ALU.add),
                         reads=[PV.b, bvb.b], writes=[vs.b])
                else:
                    P.op("v", lambda e, vs=vs, PV=PV: e.tensor_copy(vs[:, :], PV[:, 0:1024]),
                         reads=[PV.b], writes=[vs.b])
                dstap = V[st].ap.rearrange("p (g c) -> p g c", g=16)[:, :, 0:64]
                srcap = vs.ap.rearrange("p (g c) -> p g c", g=16)
                P.op("v", lambda e, dstap=dstap, srcap=srcap: e.tensor_copy(dstap, srcap),
                     reads=[vs.b], writes=[V[st].b])


def proj_featmajor(B, es, W, XN16, store_name, bias=None, scale=None):
    P = B.P
    PQ2 = [B.psum(es, f"PQp{i}", [128, 2048], F32) for i in range(2)]
    QS = [B.sbuf(es, f"QSp{i}", [128, T], BF16) for i in range(2)]
    for fo in range(NFC):
        PQ = PQ2[fo % 2]

        def mk(e, fo=fo, PQ=PQ):
            last = None
            for ec in range(NFC):
                for tc in range(NTC):
                    last = e.matmul(PQ[:, tc * TC:(tc + 1) * TC],
                                    W[ec][:, fo * 128:(fo + 1) * 128],
                                    XN16[ec][:, tc * TC:(tc + 1) * TC],
                                    start=(ec == 0), stop=(ec == NFC - 1),
                                    skip_group_check=True)
            return last
        P.op("t", mk, reads=[W[ec].b for ec in range(NFC)] + [XN16[ec].b for ec in range(NFC)],
             writes=[PQ.b])
        qs = QS[fo % 2]
        if bias is not None:
            P.op("c", lambda e, qs=qs, fo=fo, PQ=PQ: e.add(qs[:, :], PQ[:, :], bias[:, fo:fo + 1]),
                 reads=[PQ.b, bias.b], writes=[qs.b])
        elif scale is not None:
            P.op("c", lambda e, qs=qs, PQ=PQ: e.mul(qs[:, :], PQ[:, :], scale),
                 reads=[PQ.b], writes=[qs.b])
        else:
            P.op("v", lambda e, qs=qs, PQ=PQ: e.tensor_copy(qs[:, :], PQ[:, :]),
                 reads=[PQ.b], writes=[qs.b])
        P.op("s", lambda e, fo=fo, qs=qs: e.dma_start(
                out=B.dram[store_name][fo * 128:(fo + 1) * 128, :], in_=qs[:, :]),
             reads=[qs.b], writes=[B.bufs[store_name]])


def proj_v_tokmajor(B, es, W, XN16, V, ngroups, bvb=None):
    P = B.P
    for st in range(NST):
        dstap = V[st].ap.rearrange("p (g c) -> p g c", g=ngroups)[:, :, 64:65]
        P.op("v", lambda e, dstap=dstap: e.memset(dstap, 1.0), writes=[V[st].b])
    PV2 = [B.psum(es, f"PVp{i}", [128, 1024], F32) for i in range(2)]
    VS = [B.sbuf(es, f"VSp{i}", [128, 1024], F32) for i in range(2)]
    for st in range(NST):
        PV = PV2[st % 2]

        def mkv(e, st=st, PV=PV):
            last = None
            for ec in range(NFC):
                for half in range(2):
                    last = e.matmul(PV[:, half * 512:(half + 1) * 512],
                                    XN16[ec][:, st * 128:(st + 1) * 128],
                                    W[ec][:, half * 512:(half + 1) * 512],
                                    start=(ec == 0), stop=(ec == NFC - 1),
                                    skip_group_check=True)
            return last
        P.op("t", mkv, reads=[W[ec].b for ec in range(NFC)] + [XN16[ec].b for ec in range(NFC)],
             writes=[PV.b])
        vs = VS[st % 2]
        if bvb is not None:
            P.op("v", lambda e, vs=vs, PV=PV: e.tensor_tensor(vs[:, :], PV[:, :], bvb[:, :], ALU.add),
                 reads=[PV.b, bvb.b], writes=[vs.b])
        else:
            P.op("v", lambda e, vs=vs, PV=PV: e.tensor_copy(vs[:, :], PV[:, :]), reads=[PV.b], writes=[vs.b])
        dstap = V[st].ap.rearrange("p (g c) -> p g c", g=16)[:, :, 0:64]
        srcap = vs.ap.rearrange("p (g c) -> p g c", g=16)
        P.op("v", lambda e, dstap=dstap, srcap=srcap: e.tensor_copy(dstap, srcap),
             reads=[vs.b], writes=[V[st].b])



def build_layer0(B, sems):
    nc, P = B.nc, B.P
    with contextlib.ExitStack() as esL:
        esV = contextlib.ExitStack()
        V = [B.sbuf(esV, f"V0_{st}", [128, 16 * 65], BF16) for st in range(NST)]
        esMR = contextlib.ExitStack()
        with contextlib.ExitStack() as es:
            M, R = layernorm_rows(B, esMR, es, "0")
            P.barrier()
        P.flush(nc, sems)

        with contextlib.ExitStack() as esx:
            XN16 = [B.sbuf(esx, f"XN16_{i}", [128, T], BF16) for i in range(NFC)]
            with contextlib.ExitStack() as es:
                ln_apply_bf16(B, es, M, R, XN16, park_name="xn32s")
                P.barrier()
            P.flush(nc, sems)
            B.tap("xn16_0", XN16[0], [128, T], BF16)
            if B.cfg.get("l0_upto") != "ln":
                for st in range(NST):
                    dstap = V[st].ap.rearrange("p (g c) -> p g c", g=16)[:, :, 64:65]
                    P.op("v", lambda e, dstap=dstap: e.memset(dstap, 1.0), writes=[V[st].b])
                with contextlib.ExitStack() as es:
                    proj_qkv(B, es, [("L0WqT", "qTs", None), ("L0WkT", "kTs", None)],
                             XN16, V=V, vspec=("L0WvT", None))
                    P.barrier()
                P.flush(nc, sems)
        if B.cfg.get("l0_upto") in ("ln", "qkv"):
            esMR.close()
            esV.close()
            return

        esMR.close()
        attn_l0(B, esL, sems, V)
        P.barrier()
        P.flush(nc, sems)
        esV.close()
        if B.cfg.get("l0_upto") == "attn":
            return

        ffn_l0(B, esL, sems)
        P.barrier()
        P.flush(nc, sems)


def attn_l0(B, esL, sems, V):
    nc, P = B.nc, B.P
    X32 = B.consts["X32"]
    ONESC = B.consts["ONESC"]
    ONESR = B.consts["ONESR"]
    EPSA = B.consts["EPSA"]
    with contextlib.ExitStack() as es:
        LAM = B.sbuf(es, "LAM", [1, 2], F32)
        Lr = [B.sbuf(es, f"Lr{i}", [1, 32], F32) for i in range(4)]
        for i, nm in enumerate(("lamq1", "lamk1", "lamq2", "lamk2")):
            P.op("s", lambda e, nm=nm, i=i: e.dma_start(out=Lr[i][:, :], in_=B.dram[nm][:, :]),
                 reads=[B.bufs[nm]], writes=[Lr[i].b])
        P.op("v", lambda e: e.tensor_mul(Lr[0][:, :], Lr[0][:, :], Lr[1][:, :]), reads=[Lr[0].b, Lr[1].b], writes=[Lr[0].b])
        P.op("v", lambda e: e.tensor_mul(Lr[2][:, :], Lr[2][:, :], Lr[3][:, :]), reads=[Lr[2].b, Lr[3].b], writes=[Lr[2].b])
        P.op("v", lambda e: e.reduce_sum(LAM[0:1, 0:1], Lr[0][:, :], axis=AX.X), reads=[Lr[0].b], writes=[LAM.b])
        P.op("v", lambda e: e.reduce_sum(LAM[0:1, 1:2], Lr[2][:, :], axis=AX.X), reads=[Lr[2].b], writes=[LAM.b])
        P.op("c", lambda e: e.activation(LAM[:, :], LAM[:, :], AF.Exp), reads=[LAM.b], writes=[LAM.b])
        P.op("v", lambda e: e.tensor_sub(LAM[0:1, 0:1], LAM[0:1, 0:1], LAM[0:1, 1:2]), reads=[LAM.b], writes=[LAM.b])
        P.op("v", lambda e: e.tensor_scalar(LAM[0:1, 0:1], LAM[0:1, 0:1], LAMBDA_INIT, None, ALU.add),
             reads=[LAM.b], writes=[LAM.b])

        SUBL = B.sbuf(es, "SUBL", [64, 1], F32)
        P.op("s", lambda e: e.dma_start(out=SUBL[:, :], in_=B.dram["subln"][:, :]),
             reads=[B.bufs["subln"]], writes=[SUBL.b])
        P.op("v", lambda e: e.tensor_scalar(SUBL[:, :], SUBL[:, :], 1.0 - LAMBDA_INIT, None, ALU.mult),
             reads=[SUBL.b], writes=[SUBL.b])

        AO = [B.sbuf(es, f"AO{i}", [128, T], BF16) for i in range(NFC)]

        with contextlib.ExitStack() as ep:
            QP = [B.sbuf(ep, f"QP{i}", [64, T], BF16) for i in range(2)]
            KP = [B.sbuf(ep, f"KP{i}", [64, T], BF16) for i in range(2)]
            E0 = [B.sbuf(ep, f"E0{i}", [128, 1024], BF16) for i in range(2)]
            E1 = E0
            ATTS = [B.sbuf(ep, f"ATTS{i}", [64, TC], BF16) for i in range(1)]
            SEG2 = B.sbuf(ep, "SEG2", [64, 2], BF16)
            P.op("v", lambda e: e.memset(SEG2[:, :], 0.0), writes=[SEG2.b])
            P.op("v", lambda e: e.memset(SEG2[0:32, 0:1], 1.0), writes=[SEG2.b])
            P.op("v", lambda e: e.memset(SEG2[32:64, 1:2], 1.0), writes=[SEG2.b])
            IDs = B.sbuf(ep, "IDs", [128, 128], F32)
            P.op("s", lambda e: e.dma_start(out=IDs[:, :], in_=B.dram["ID128"][:, :]),
                 reads=[B.bufs["ID128"]], writes=[IDs.b])
            QN2 = B.sbuf(ep, "QN2", [2, 4], F32)
            HB = B.sbuf(ep, "HBr", [1, TC], BF16)
            LB = B.sbuf(ep, "LBr", [1, TC], BF16)
            DF = B.sbuf(ep, "DFr", [1, TC], F32)
            RAW0s = [B.sbuf(ep, f"RAW0_{i}", [64, T], F32) for i in range(2)]
            RAW1s = [B.sbuf(ep, f"RAW1_{i}", [64, T], F32) for i in range(2)]
            ZALL = B.sbuf(ep, "ZALL", [128, T], F32)
            PZ = (0, 32, 64, 96)
            ZB = [Buf(f"ZB{r}") for r in range(4)]
            RR = B.sbuf(ep, "RRrow", [1, T], F32)
            SQA = B.sbuf(ep, "SQA", [64, TC], F32)
            SQ16 = [B.sbuf(ep, f"SQ16_{i}", [64, TC], BF16) for i in range(2)]
            NRM = B.sbuf(ep, "NRM", [1, 8], F32)
            NEGCC = B.sbuf(ep, "NEGCC", [128, 2], F32)
            PS2 = [B.psum(ep, f"PS0_{i}", [128, 1024], F32) for i in range(2)]
            PA0 = [B.psum(ep, f"PA0_{i}", [65, TC], F32) for i in range(2)]
            PA1 = PA0
            PRW = B.psum(ep, "PRW", [1, TC], F32)
            PBC = B.psum(ep, "PBC", [128, TC], F32)

            def combine_stages(p):
                RAW0, RAW1 = RAW0s[p % 2], RAW1s[p % 2]
                z0, z1 = 2 * (p % 2), 2 * (p % 2) + 1

                def zmul(tc):
                    tsl = slice(tc * TC, (tc + 1) * TC)
                    for (zr, raw) in ((z0, RAW0), (z1, RAW1)):
                        zp = PZ[zr]
                        sc = LAM[0:1, 0:1] if zr == z1 else None
                        bcast_hl(B, HB, LB, DF, ZALL[zp:zp + 1, tsl], PBC[0:64, :], 64,
                                 [ZB[zr]], PBC.b, scale_ap=sc,
                                 scale_reads=[LAM.b] if sc is not None else ())
                        P.op("v", lambda e, tsl=tsl, raw=raw: e.tensor_tensor(
                                raw[:, tsl], raw[:, tsl], PBC[0:64, :], ALU.mult),
                             reads=[PBC.b, raw.b], writes=[raw.b])

                def s1():
                    for zr in (z0, z1):
                        zp = PZ[zr]
                        P.op("c", lambda e, zp=zp: e.activation(ZALL[zp:zp + 1, :], ZALL[zp:zp + 1, :], AF.Ln),
                             reads=[ZB[zr]], writes=[ZB[zr]])
                        P.op("c", lambda e, zp=zp: e.activation(ZALL[zp:zp + 1, :], ZALL[zp:zp + 1, :], AF.Exp,
                                                                scale=-1.0),
                             reads=[ZB[zr]], writes=[ZB[zr]])
                    zmul(0)
                    zmul(1)

                def s2():
                    zmul(2)
                    zmul(3)
                    P.op("v", lambda e: e.tensor_sub(RAW0[:, :], RAW0[:, :], RAW1[:, :]),
                         reads=[RAW0.b, RAW1.b], writes=[RAW0.b])

                def s3():
                    for tc in range(NTC):
                        tsl = slice(tc * TC, (tc + 1) * TC)
                        P.op("c", lambda e, tsl=tsl: e.activation(SQA[:, :], RAW0[:, tsl], AF.Square),
                             reads=[RAW0.b], writes=[SQA.b])
                        P.op("t", lambda e: e.matmul(PRW[0:1, :], ONESC[0:64, :], SQA[:, :],
                                                     start=True, stop=True),
                             reads=[SQA.b, ONESC.b], writes=[PRW.b])
                        P.op("c", lambda e, tsl=tsl: e.activation(RR[0:1, tsl], PRW[0:1, :], AF.Ln,
                                                                  bias=EPSA[0:1, :], scale=1.0 / 64),
                             reads=[PRW.b, EPSA.b], writes=[RR.b])
                    P.op("c", lambda e: e.activation(RR[0:1, :], RR[0:1, :], AF.Exp, scale=-0.5),
                         reads=[RR.b], writes=[RR.b])

                def s4():
                    ftile = p // 2
                    rbase = 64 * (p % 2)
                    for tc in range(NTC):
                        tsl = slice(tc * TC, (tc + 1) * TC)
                        bcast_hl(B, HB, LB, DF, RR[0:1, tsl], PBC[0:64, :], 64, [RR.b], PBC.b)
                        P.op("v", lambda e, tsl=tsl: e.tensor_tensor(RAW0[:, tsl], RAW0[:, tsl],
                                                                     PBC[0:64, :], ALU.mult),
                             reads=[PBC.b, RAW0.b], writes=[RAW0.b])
                        if rbase == 0:
                            P.op("v", lambda e, ftile=ftile, tsl=tsl: e.tensor_scalar(
                                    AO[ftile][0:64, tsl], RAW0[:, tsl], SUBL[:, :], None, ALU.mult),
                                 reads=[RAW0.b, SUBL.b], writes=[AO[ftile].b])
                        else:
                            ats = ATTS[0]
                            P.op("v", lambda e, ats=ats, tsl=tsl: e.tensor_scalar(
                                    ats[:, :], RAW0[:, tsl], SUBL[:, :], None, ALU.mult),
                                 reads=[RAW0.b, SUBL.b], writes=[ats.b])
                            P.op("g", lambda e, ftile=ftile, tsl=tsl, ats=ats: e.dma_start(
                                    out=AO[ftile][64:128, tsl], in_=ats[:, :]),
                                 reads=[ats.b], writes=[AO[ftile].b])

                return [s1, s2, s3, s4]

            for p in range(16):
                qp, kp = QP[p % 2], KP[p % 2]
                if p == 0:
                    P.op("g", lambda e, qp=qp: e.dma_start(out=qp[:, :], in_=B.dram["qTs"][0:64, :]),
                         reads=[B.bufs["qTs"]], writes=[qp.b])
                    P.op("g", lambda e, kp=kp: e.dma_start(out=kp[:, :], in_=B.dram["kTs"][0:64, :]),
                         reads=[B.bufs["kTs"]], writes=[kp.b])
                for src, o in ((qp, 0), (kp, 1)):
                    for tc in range(NTC):
                        sq = SQ16[tc % 2]
                        P.op("c", lambda e, src=src, tc=tc, sq=sq: e.activation(
                                sq[:, :], src[:, tc * TC:(tc + 1) * TC], AF.Square),
                             reads=[src.b], writes=[sq.b])
                        ps_ = PS2[tc // 2]
                        P.op("t", lambda e, tc=tc, ps_=ps_, sq=sq: e.matmul(
                                ps_[0:2, (tc % 2) * TC:((tc % 2) + 1) * TC],
                                SEG2[:, :], sq[:, :],
                                start=True, stop=True, skip_group_check=True),
                             reads=[sq.b, SEG2.b], writes=[ps_.b])
                    P.op("v", lambda e, o=o: e.reduce_max(QN2[0:2, 2:3], PS2[0][0:2, :], axis=AX.X),
                         reads=[PS2[0].b], writes=[QN2.b])
                    P.op("v", lambda e, o=o: e.reduce_max(QN2[0:2, 3:4], PS2[1][0:2, :], axis=AX.X),
                         reads=[PS2[1].b], writes=[QN2.b])
                    P.op("v", lambda e, o=o: e.tensor_max(QN2[0:2, o:o + 1], QN2[0:2, 2:3], QN2[0:2, 3:4]),
                         reads=[QN2.b], writes=[QN2.b])
                P.op("v", lambda e: e.tensor_tensor(QN2[0:2, 2:3], QN2[0:2, 0:1], QN2[0:2, 1:2], ALU.mult),
                     reads=[QN2.b], writes=[QN2.b])
                P.op("t", lambda e: e.transpose(PRW[0:1, 0:2], QN2[0:2, 2:3], IDs[0:2, 0:2]),
                     reads=[QN2.b, IDs.b], writes=[PRW.b])
                P.op("v", lambda e: e.tensor_copy(NRM[0:1, 4:6], PRW[0:1, 0:2]),
                     reads=[PRW.b], writes=[NRM.b])
                P.op("c", lambda e: e.activation(NRM[0:1, 4:6], NRM[0:1, 4:6], AF.Ln), reads=[NRM.b], writes=[NRM.b])
                P.op("c", lambda e: e.activation(NRM[0:1, 4:6], NRM[0:1, 4:6], AF.Exp, scale=0.5),
                     reads=[NRM.b], writes=[NRM.b])
                P.op("v", lambda e: e.tensor_scalar(NRM[0:1, 4:6], NRM[0:1, 4:6], -SC0, None, ALU.mult),
                     reads=[NRM.b], writes=[NRM.b])
                for j in range(2):
                    P.op("t", lambda e, j=j: e.matmul(PBC[:, 0:1], ONESR[:, :], NRM[0:1, 4 + j:5 + j],
                                                      start=True, stop=True),
                         reads=[NRM.b, ONESR.b], writes=[PBC.b])
                    P.op("v", lambda e, j=j: e.tensor_copy(NEGCC[:, j:j + 1], PBC[:, 0:1]),
                         reads=[PBC.b], writes=[NEGCC.b])
                if p < 15:
                    qn, kn = QP[(p + 1) % 2], KP[(p + 1) % 2]
                    P.op("g", lambda e, p=p, qn=qn: e.dma_start(
                            out=qn[:, :], in_=B.dram["qTs"][(p + 1) * 64:(p + 2) * 64, :]),
                         reads=[B.bufs["qTs"]], writes=[qn.b])
                    P.op("g", lambda e, p=p, kn=kn: e.dma_start(
                            out=kn[:, :], in_=B.dram["kTs"][(p + 1) * 64:(p + 2) * 64, :]),
                         reads=[B.bufs["kTs"]], writes=[kn.b])
                stages = combine_stages(p - 1) if p > 0 else []
                bi = 0

                for tcp in range(2):
                    t0sl = slice((2 * tcp) * TC, (2 * tcp + 1) * TC)
                    t1sl = slice((2 * tcp + 1) * TC, (2 * tcp + 2) * TC)
                    for hj in range(2):
                        base = 32 * hj
                        ebufs = E0 if hj == 0 else E1
                        pa = PA0 if hj == 0 else PA1

                        def emit_av0(sp, pa=pa, p=p, ebufs=ebufs):
                            def mkav(e, sp=sp, p=p, pa=pa, ebufs=ebufs):
                                last = None
                                for si in (2 * sp, 2 * sp + 1):
                                    eb = ebufs[si % 2]
                                    for q in range(2):
                                        last = e.matmul(pa[q][0:65, :],
                                                        V[si][:, p * 65:(p + 1) * 65],
                                                        eb[:, q * TC:(q + 1) * TC],
                                                        start=(si == 0), stop=(si == 15),
                                                        skip_group_check=True)
                                return last
                            P.op("t", mkav,
                                 reads=[ebufs[0].b, ebufs[1].b, V[2 * sp].b, V[2 * sp + 1].b],
                                 writes=[pa[0].b, pa[1].b])

                        for sp in range(8):
                            def mksc(e, sp=sp, base=base, kp=kp, qp=qp, t0sl=t0sl, t1sl=t1sl):
                                last = None
                                for si in (2 * sp, 2 * sp + 1):
                                    ps_ = PS2[si % 2]
                                    for tsl, half in ((t0sl, 0), (t1sl, 1)):
                                        last = e.matmul(ps_[:, half * TC:(half + 1) * TC],
                                                        kp[base:base + 32, si * 128:(si + 1) * 128],
                                                        qp[base:base + 32, tsl],
                                                        start=True, stop=True, tile_position=(base, 0),
                                                        skip_group_check=True)
                                return last
                            P.op("t", mksc, reads=[kp.b, qp.b], writes=[PS2[0].b, PS2[1].b])
                            if sp > 0:
                                emit_av0(sp - 1)
                            for si in (2 * sp, 2 * sp + 1):
                                ps_ = PS2[si % 2]
                                ebuf = ebufs[si % 2]
                                P.op("c", lambda e, hj=hj, ebuf=ebuf, ps_=ps_: e.activation(
                                        ebuf[:, :], ps_[:, :], AF.Exp, bias=NEGCC[:, hj:hj + 1], scale=SC0),
                                     reads=[ps_.b, NEGCC.b], writes=[ebuf.b])
                        emit_av0(7)
                        raw = (RAW0s if hj == 0 else RAW1s)[p % 2]
                        zr = 2 * (p % 2) + hj
                        zp = PZ[zr]
                        for q, tsl in ((0, t0sl), (1, t1sl)):
                            P.op("v", lambda e, tsl=tsl, q=q, raw=raw, pa=pa: e.tensor_copy(
                                    raw[:, tsl], pa[q][0:64, :]),
                                 reads=[pa[q].b], writes=[raw.b])
                            P.op("v", lambda e, tsl=tsl, q=q, zp=zp, pa=pa: e.tensor_copy(
                                    ZALL[zp:zp + 1, tsl], pa[q][64:65, :]),
                                 reads=[pa[q].b], writes=[ZB[zr]])
                        if bi < len(stages):
                            stages[bi]()
                            bi += 1
            for st in combine_stages(15):
                st()
            P.barrier()

        with contextlib.ExitStack() as ew:
            W = load_w_bf16(B, ew, "L0WoT", D, D, "wo")
            PW = [B.psum(ew, f"PWo{i}", [128, 2048], F32) for i in range(2)]
            for fo in range(NFC):
                pw = PW[fo % 2]

                def mkwo(e, fo=fo, pw=pw):
                    last = None
                    for ec in range(NFC):
                        for tc in range(NTC):
                            last = e.matmul(pw[:, tc * TC:(tc + 1) * TC],
                                            W[ec][:, fo * 128:(fo + 1) * 128],
                                            AO[ec][:, tc * TC:(tc + 1) * TC],
                                            start=(ec == 0), stop=(ec == NFC - 1),
                                            skip_group_check=True)
                    return last
                P.op("t", mkwo, reads=[W[ec].b for ec in range(NFC)] + [AO[ec].b for ec in range(NFC)],
                     writes=[pw.b])
                for tc in range(NTC):
                    tsl = slice(tc * TC, (tc + 1) * TC)
                    P.op("v", lambda e, fo=fo, tsl=tsl, pw=pw, tc=tc: e.tensor_tensor(
                            X32[fo][:, tsl], X32[fo][:, tsl], pw[:, tc * TC:(tc + 1) * TC], ALU.add),
                         reads=[pw.b, X32[fo].b], writes=[X32[fo].b])
            P.barrier()

def ffn_l0(B, esL, sems):
    nc, P = B.nc, B.P
    X32 = B.consts["X32"]
    ONESR = B.consts["ONESR"]
    with contextlib.ExitStack() as es:
        OUT32 = [B.sbuf(es, f"OUT32_{i}", [128, T], F32) for i in range(NFC)]
        OB = {(i, tc): Buf(f"OB{i}_{tc}") for i in range(NFC) for tc in range(NTC)}
        O16 = [B.sbuf(es, f"O16_{i}", [128, TC], BF16) for i in range(NFC)]
        for i in range(NFC):
            P.op("s", lambda e, i=i: e.dma_start(out=OUT32[i][:, :], in_=B.dram["xn32s"][i * 128:(i + 1) * 128, :]),
                 reads=[B.bufs["xn32s"]], writes=[OB[(i, tc)] for tc in range(NTC)])
        CW1 = load_w_bf16(B, es, "cnw1", D, D, "c1")
        CW2 = load_w_bf16(B, es, "cnw2", D, 512, "c2")
        CW3 = load_w_bf16(B, es, "cnw3", 512, 256, "c3")
        CW4 = load_w_bf16(B, es, "cnw4", 256, 1, "c4")
        BIAS = {}
        for nm, shape in (("cnb1", [128, 8]), ("cnb2", [128, 4]), ("cnb3", [128, 2]), ("cnb4", [1, 1]),
                          ("gcb1", [128, 4]), ("gcb2", [128, 2]), ("gcb3", [1, 1]), ("arange", [1, 1])):
            t = B.sbuf(es, "sb_" + nm, shape, F32)
            P.op("s", lambda e, nm=nm, t=t: e.dma_start(out=t[:, :], in_=B.dram[nm][:, :]),
                 reads=[B.bufs[nm]], writes=[t.b])
            BIAS[nm] = t
        CC = B.sbuf(es, "CC", [1, 2], F32)
        P.op("v", lambda e: e.tensor_scalar(CC[0:1, 0:1], BIAS["arange"][0:1, :], 2.0, None, ALU.mult),
             reads=[BIAS["arange"].b], writes=[CC.b])
        P.op("v", lambda e: e.tensor_scalar(CC[0:1, 1:2], BIAS["arange"][0:1, :], -1.0, 0.1, ALU.mult, ALU.add),
             reads=[BIAS["arange"].b], writes=[CC.b])

        GF = B.sbuf(es, "GF", [1, 2], F32)
        H1 = [B.sbuf(es, f"H1_{i}", [128, TC], BF16) for i in range(NFC)]
        H2 = [B.sbuf(es, f"H2_{i}", [128, TC], BF16) for i in range(4)]
        H3 = [B.sbuf(es, f"H3_{i}", [128, TC], BF16) for i in range(2)]
        TF = B.sbuf(es, "TFROW", [1, TC], F32)
        HBf = B.sbuf(es, "HBf", [1, TC], BF16)
        LBf = B.sbuf(es, "LBf", [1, TC], BF16)
        DFf = B.sbuf(es, "DFf", [1, TC], F32)
        TW = [B.sbuf(es, f"TW{i}", [128, TC], F32) for i in range(4)]
        PH = [B.psum(es, f"PH{i}", [128, TC], F32) for i in range(2)]
        PCP2 = [B.psum(es, f"PCP{i}", [128, TC], F32) for i in range(2)]
        PR2 = B.psum(es, "PR2", [1, TC], F32)

        esG = contextlib.ExitStack()
        GW1 = load_w_bf16(B, esG, "gcw1", D, 512, "g1")
        GW2 = load_w_bf16(B, esG, "gcw2", 512, 256, "g2")
        GW3 = load_w_bf16(B, esG, "gcw3", 256, 1, "g3")
        GC = B.sbuf(esG, "GC", [128, 8], F32)
        GC16 = B.sbuf(esG, "GC16", [128, 8], BF16)
        HG1 = B.sbuf(esG, "HG1", [128, 4], BF16)
        HG2 = B.sbuf(esG, "HG2", [128, 2], BF16)
        PG = B.psum(esG, "PG", [128, 8], F32)
        for i in range(NFC):
            P.op("v", lambda e, i=i: e.reduce_sum(GC[:, i:i + 1], OUT32[i][:, :], axis=AX.X),
                 reads=[OB[(i, tc)] for tc in range(NTC)], writes=[GC.b])
        P.op("v", lambda e: e.tensor_scalar(GC[:, :], GC[:, :], 1.0 / T, None, ALU.mult),
             reads=[GC.b], writes=[GC.b])
        P.op("v", lambda e: e.tensor_copy(GC16[:, :], GC[:, :]), reads=[GC.b], writes=[GC16.b])

        def mkg1(e):
            last = None
            for fo in range(4):
                for ec in range(NFC):
                    last = e.matmul(PG[:, fo:fo + 1], GW1[ec][:, fo * 128:(fo + 1) * 128],
                                    GC16[:, ec:ec + 1], start=(ec == 0), stop=(ec == NFC - 1),
                                    skip_group_check=True)
            return last
        P.op("t", mkg1, reads=[GC16.b] + [w.b for w in GW1], writes=[PG.b])
        for fo in range(4):
            P.op("c", lambda e, fo=fo: e.activation(HG1[:, fo:fo + 1], PG[:, fo:fo + 1], AF.Gelu,
                                                    bias=BIAS["gcb1"][:, fo:fo + 1]),
                 reads=[PG.b, BIAS["gcb1"].b], writes=[HG1.b])

        def mkg2(e):
            last = None
            for fo in range(2):
                for ec in range(4):
                    last = e.matmul(PG[:, 4 + fo:5 + fo], GW2[ec][:, fo * 128:(fo + 1) * 128],
                                    HG1[:, ec:ec + 1], start=(ec == 0), stop=(ec == 3),
                                    skip_group_check=True)
            return last
        P.op("t", mkg2, reads=[HG1.b] + [w.b for w in GW2], writes=[PG.b])
        for fo in range(2):
            P.op("c", lambda e, fo=fo: e.activation(HG2[:, fo:fo + 1], PG[:, 4 + fo:5 + fo], AF.Gelu,
                                                    bias=BIAS["gcb2"][:, fo:fo + 1]),
                 reads=[PG.b, BIAS["gcb2"].b], writes=[HG2.b])

        def mkg3(e):
            last = None
            for ec in range(2):
                last = e.matmul(PG[0:1, 6:7], GW3[ec][:, 0:1], HG2[:, ec:ec + 1],
                                start=(ec == 0), stop=(ec == 1), skip_group_check=True)
            return last
        P.op("t", mkg3, reads=[HG2.b] + [w.b for w in GW3], writes=[PG.b])
        P.op("c", lambda e: e.activation(GF[0:1, 0:1], PG[0:1, 6:7], AF.Sigmoid,
                                         bias=BIAS["gcb3"][0:1, :]),
             reads=[PG.b, BIAS["gcb3"].b], writes=[GF.b])
        P.op("v", lambda e: e.tensor_scalar(GF[0:1, 1:2], GF[0:1, 0:1], 0.7, None, ALU.mult),
             reads=[GF.b], writes=[GF.b])
        P.barrier()
        esG.close()
        B.tap("gf", GF, [1, 2], F32)
        if B.cfg.get("l0_upto") == "ffngf":
            return

        twist_pairs = [(0, 4, 1.0), (1, 5, 1.0), (2, 6, -1.0), (3, 7, -1.0)]

        ncyc = {"ffncyc1": 1, "ffncyc2": 2}.get(B.cfg.get("l0_upto"), 3)
        for cyc in range(ncyc):
            for tc in range(NTC):
                tsl = slice(tc * TC, (tc + 1) * TC)
                for i in range(NFC):
                    P.op("v", lambda e, i=i, tsl=tsl: e.tensor_copy(O16[i][:, :], OUT32[i][:, tsl]),
                         reads=[OB[(i, tc)]], writes=[O16[i].b])
                for fo in range(NFC):
                    ph = PH[fo % 2]

                    def mk1(e, fo=fo, ph=ph):
                        last = None
                        for ec in range(NFC):
                            last = e.matmul(ph[:, :], CW1[ec][:, fo * 128:(fo + 1) * 128],
                                            O16[ec][:, :], start=(ec == 0), stop=(ec == NFC - 1))
                        return last
                    P.op("t", mk1, reads=[w.b for w in CW1] + [O16[ec].b for ec in range(NFC)], writes=[ph.b])
                    P.op("c", lambda e, fo=fo, ph=ph: e.activation(H1[fo][:, :], ph[:, :], AF.Gelu,
                                                                   bias=BIAS["cnb1"][:, fo:fo + 1]),
                         reads=[ph.b, BIAS["cnb1"].b], writes=[H1[fo].b])
                for fo in range(4):
                    ph = PH[fo % 2]

                    def mk2(e, fo=fo, ph=ph):
                        last = None
                        for ec in range(NFC):
                            last = e.matmul(ph[:, :], CW2[ec][:, fo * 128:(fo + 1) * 128],
                                            H1[ec][:, :], start=(ec == 0), stop=(ec == NFC - 1))
                        return last
                    P.op("t", mk2, reads=[w.b for w in CW2] + [h.b for h in H1], writes=[ph.b])
                    P.op("c", lambda e, fo=fo, ph=ph: e.activation(H2[fo][:, :], ph[:, :], AF.Gelu,
                                                                   bias=BIAS["cnb2"][:, fo:fo + 1]),
                         reads=[ph.b, BIAS["cnb2"].b], writes=[H2[fo].b])
                for fo in range(2):
                    ph = PH[fo % 2]

                    def mk3(e, fo=fo, ph=ph):
                        last = None
                        for ec in range(4):
                            last = e.matmul(ph[:, :], CW3[ec][:, fo * 128:(fo + 1) * 128],
                                            H2[ec][:, :], start=(ec == 0), stop=(ec == 3))
                        return last
                    P.op("t", mk3, reads=[w.b for w in CW3] + [h.b for h in H2], writes=[ph.b])
                    P.op("c", lambda e, fo=fo, ph=ph: e.activation(H3[fo][:, :], ph[:, :], AF.Gelu,
                                                                   bias=BIAS["cnb3"][:, fo:fo + 1]),
                         reads=[ph.b, BIAS["cnb3"].b], writes=[H3[fo].b])

                def mk4(e):
                    last = None
                    for ec in range(2):
                        last = e.matmul(PR2[0:1, :], CW4[ec][:, 0:1], H3[ec][:, :],
                                        start=(ec == 0), stop=(ec == 1))
                    return last
                P.op("t", mk4, reads=[w.b for w in CW4] + [h.b for h in H3], writes=[PR2.b])
                P.op("c", lambda e: e.activation(TF[0:1, :], PR2[0:1, :], AF.Sigmoid,
                                                 bias=BIAS["cnb4"][0:1, :]),
                     reads=[PR2.b, BIAS["cnb4"].b], writes=[TF.b])
                P.op("v", lambda e: e.tensor_scalar(TF[0:1, :], TF[0:1, :], 0.3, GF[0:1, 1:2], ALU.mult, ALU.add),
                     reads=[TF.b, GF.b], writes=[TF.b])
                P.op("v", lambda e: e.tensor_scalar(TF[0:1, :], TF[0:1, :], CC[0:1, 0:1], CC[0:1, 1:2],
                                                    ALU.mult, ALU.add),
                     reads=[TF.b, CC.b], writes=[TF.b])
                PCP = PCP2[tc % 2]
                bcast_hl(B, HBf, LBf, DFf, TF[0:1, :], PCP[:, :], 128, [TF.b], PCP.b)
                tw0 = TW[2 * (tc % 2)]
                tw1 = TW[2 * (tc % 2) + 1]
                for (a, bb, sgn) in twist_pairs:
                    P.op("v", lambda e, bb=bb, tsl=tsl, tw0=tw0, PCP=PCP: e.tensor_tensor(
                            tw0[:, :], OUT32[bb][:, tsl], PCP[:, :], ALU.mult),
                         reads=[OB[(bb, tc)], PCP.b], writes=[tw0.b])
                    P.op("v", lambda e, a=a, tsl=tsl, tw1=tw1, PCP=PCP: e.tensor_tensor(
                            tw1[:, :], OUT32[a][:, tsl], PCP[:, :], ALU.mult),
                         reads=[OB[(a, tc)], PCP.b], writes=[tw1.b])
                    op_a = ALU.add if sgn > 0 else ALU.subtract
                    op_b = ALU.subtract if sgn > 0 else ALU.add
                    P.op("v", lambda e, a=a, tsl=tsl, op_a=op_a, tw0=tw0: e.tensor_tensor(
                            OUT32[a][:, tsl], OUT32[a][:, tsl], tw0[:, :], op_a),
                         reads=[tw0.b, OB[(a, tc)]], writes=[OB[(a, tc)]])
                    P.op("v", lambda e, bb=bb, tsl=tsl, op_b=op_b, tw1=tw1: e.tensor_tensor(
                            OUT32[bb][:, tsl], OUT32[bb][:, tsl], tw1[:, :], op_b),
                         reads=[tw1.b, OB[(bb, tc)]], writes=[OB[(bb, tc)]])
        for i in range(NFC):
            P.op("v", lambda e, i=i: e.tensor_tensor(X32[i][:, :], X32[i][:, :], OUT32[i][:, :], ALU.add),
                 reads=[OB[(i, tc)] for tc in range(NTC)] + [X32[i].b], writes=[X32[i].b])



def build_layer1(B, sems):
    nc, P = B.nc, B.P
    X32 = B.consts["X32"]
    ONESR = B.consts["ONESR"]
    with contextlib.ExitStack() as esL:
        esV = contextlib.ExitStack()
        V = [B.sbuf(esV, f"V1_{st}", [128, 16 * 65], BF16) for st in range(NST)]
        esX = contextlib.ExitStack()
        XN16 = [B.sbuf(esX, f"XN16b_{i}", [128, T], BF16) for i in range(NFC)]
        GR8 = B.sbuf(esX, "GR8", [8, T], F32)
        esMR = contextlib.ExitStack()
        with contextlib.ExitStack() as es:
            M, R = layernorm_rows(B, esMR, es, "1")
            P.barrier()
        P.flush(nc, sems)

        with contextlib.ExitStack() as es:
            GW32 = B.sbuf(es, "GW32", [128, 64], F32)
            src = B.dram["gwT"].rearrange("(ec p) c -> p ec c", p=128)
            P.op("s", lambda e, src=src: e.dma_start(
                    out=GW32.ap.rearrange("p (ec c) -> p ec c", ec=8), in_=src),
                 reads=[B.bufs["gwT"]], writes=[GW32.b])
            IOT = B.sbuf(es, "IOT", [128, 8], F32)
            P.op("s", lambda e: e.dma_start(out=IOT[:, :], in_=B.dram["IOTA8"][:, :]),
                 reads=[B.bufs["IOTA8"]], writes=[IOT.b])
            IDM = B.sbuf(es, "IDM", [128, 128], F32)
            P.op("s", lambda e: e.dma_start(out=IDM[:, :], in_=B.dram["ID128"][:, :]),
                 reads=[B.bufs["ID128"]], writes=[IDM.b])
            GL = B.sbuf(es, "GL", [128, 8], F32)
            MX = B.sbuf(es, "MX", [128, 8], F32)
            MI = B.sbuf(es, "MI", [128, 8], U32)
            MIF = B.sbuf(es, "MIF", [128, 2], F32)
            GSC = B.sbuf(es, "GSC", [128, 4], F32)
            OH = [B.sbuf(es, f"OH{i}", [128, 8], F32) for i in range(2)]
            GT = B.sbuf(es, "GT", [128, 8], F32)
            G4 = B.psum(es, "G4", [128, 32], F32)
            PT8 = B.psum(es, "PT8", [8, 128], F32)

            def gate_cb(tc, XA):
                def mkgl(e, tc=tc):
                    last = None
                    for j in range(4):
                        for ec in range(NFC):
                            last = e.matmul(G4[:, j * 8:(j + 1) * 8],
                                            XA[ec][:, j * 128:(j + 1) * 128],
                                            GW32[:, ec * 8:(ec + 1) * 8],
                                            start=(ec == 0), stop=(ec == NFC - 1),
                                            skip_group_check=True)
                    return last
                P.op("t", mkgl, reads=[xa.b for xa in XA] + [GW32.b], writes=[G4.b])
                for j in range(4):
                    tt = tc * 4 + j
                    P.op("v", lambda e, j=j: e.tensor_copy(GL[:, :], G4[:, j * 8:(j + 1) * 8]),
                         reads=[G4.b], writes=[GL.b])
                    P.op("v", lambda e: e.max(MX[:, :], GL[:, :]), reads=[GL.b], writes=[MX.b])
                    P.op("v", lambda e: e.max_index(MI[:, :], MX[:, :], GL[:, :]),
                         reads=[MX.b, GL.b], writes=[MI.b])
                    P.op("v", lambda e: e.tensor_copy(MIF[:, :], MI[:, 0:2]), reads=[MI.b], writes=[MIF.b])
                    P.op("v", lambda e: e.tensor_sub(GSC[:, 0:1], MX[:, 0:1], MX[:, 1:2]),
                         reads=[MX.b], writes=[GSC.b])
                    P.op("c", lambda e: e.activation(GSC[:, 2:3], GSC[:, 0:1], AF.Sigmoid),
                         reads=[GSC.b], writes=[GSC.b])
                    P.op("v", lambda e: e.tensor_scalar(GSC[:, 3:4], GSC[:, 2:3], -1.0, 1.0, ALU.mult, ALU.add),
                         reads=[GSC.b], writes=[GSC.b])
                    P.op("v", lambda e: e.tensor_scalar(OH[0][:, :], IOT[:, :], MIF[:, 0:1], None, ALU.is_equal),
                         reads=[IOT.b, MIF.b], writes=[OH[0].b])
                    P.op("v", lambda e: e.tensor_scalar(OH[1][:, :], IOT[:, :], MIF[:, 1:2], None, ALU.is_equal),
                         reads=[IOT.b, MIF.b], writes=[OH[1].b])
                    P.op("v", lambda e: e.tensor_scalar(OH[0][:, :], OH[0][:, :], GSC[:, 2:3], None, ALU.mult),
                         reads=[OH[0].b, GSC.b], writes=[OH[0].b])
                    P.op("v", lambda e: e.tensor_scalar(GT[:, :], OH[1][:, :], GSC[:, 3:4], None, ALU.mult),
                         reads=[OH[1].b, GSC.b], writes=[GT.b])
                    P.op("v", lambda e: e.tensor_add(GT[:, :], GT[:, :], OH[0][:, :]),
                         reads=[GT.b, OH[0].b], writes=[GT.b])
                    P.op("t", lambda e: e.transpose(PT8[0:8, :], GT[:, :], IDM[:, :]),
                         reads=[GT.b, IDM.b], writes=[PT8.b])
                    P.op("v", lambda e, tt=tt: e.tensor_copy(GR8[0:8, tt * 128:(tt + 1) * 128], PT8[0:8, :]),
                         reads=[PT8.b], writes=[GR8.b])

            ln_apply_bf16(B, es, M, R, XN16, gate_cb=gate_cb)
            P.op("s", lambda e: e.dma_start(out=B.dram["grows"][:, :], in_=GR8[0:8, :]),
                 reads=[GR8.b], writes=[B.bufs["grows"]])
            P.barrier()
        P.flush(nc, sems)
        esMR.close()
        if B.cfg.get("l1_upto") == "ln":
            esMR.close() if False else None
            esX.close()
            esV.close()
            return

        for st in range(NST):
            dstap = V[st].ap.rearrange("p (g c) -> p g c", g=16)[:, :, 64:65]
            P.op("v", lambda e, dstap=dstap: e.memset(dstap, 1.0), writes=[V[st].b])
        with contextlib.ExitStack() as es:
            BQ = B.sbuf(es, "BQs", [128, 8], F32)
            P.op("s", lambda e: e.dma_start(out=BQ[:, :], in_=B.dram["bq"][:, :]),
                 reads=[B.bufs["bq"]], writes=[BQ.b])
            BK = B.sbuf(es, "BKs", [128, 8], F32)
            P.op("s", lambda e: e.dma_start(out=BK[:, :], in_=B.dram["bk"][:, :]),
                 reads=[B.bufs["bk"]], writes=[BK.b])
            BVR = B.sbuf(es, "BVR", [1, D], F32)
            P.op("s", lambda e: e.dma_start(out=BVR[:, :], in_=B.dram["bv"][:, :]),
                 reads=[B.bufs["bv"]], writes=[BVR.b])
            BVB = B.sbuf(es, "BVB", [128, D], F32)
            HBv = B.sbuf(es, "HBv", [1, TC], BF16)
            LBv = B.sbuf(es, "LBv", [1, TC], BF16)
            DFv = B.sbuf(es, "DFv", [1, TC], F32)
            with contextlib.ExitStack() as ebv:
                PBV = B.psum(ebv, "PBV", [128, TC], F32)
                for hh in range(2):
                    bcast_hl(B, HBv, LBv, DFv, BVR[0:1, hh * 512:(hh + 1) * 512], PBV[:, :], 128,
                             [BVR.b], PBV.b)
                    P.op("v", lambda e, hh=hh: e.tensor_copy(BVB[:, hh * 512:(hh + 1) * 512], PBV[:, :]),
                         reads=[PBV.b], writes=[BVB.b])
                P.barrier()
            proj_qkv(B, es, [("L1Wq", "qTs", BQ), ("L1Wk", "kTs", BK)],
                     XN16, V=V, vspec=("L1Wv", BVB))
            P.barrier()
        P.flush(nc, sems)
        for i in range(NFC):
            P.op("s", lambda e, i=i: e.dma_start(out=B.dram["xn16s"][i * 128:(i + 1) * 128, :],
                                                 in_=XN16[i][:, :]),
                 reads=[XN16[i].b], writes=[B.bufs["xn16s"]])
        P.barrier()
        P.flush(nc, sems)
        esX.close()
        if B.cfg.get("l1_upto") == "qkv":
            esV.close()
            return

        attn_l1(B, esL, sems, V)
        P.barrier()
        P.flush(nc, sems)
        esV.close()
        if B.cfg.get("l1_upto") == "attn":
            return

        X32 = B.consts["X32"]
        for i in range(NFC):
            P.op("s", lambda e, i=i: e.dma_start(out=B.dram["x32s"][i * 128:(i + 1) * 128, :],
                                                 in_=X32[i][:, :]),
                 reads=[X32[i].b], writes=[B.bufs["x32s"]])
        P.barrier()
        P.flush(nc, sems)
        B.esX32.close()
        B.x32_parked = True

        moe_l1(B, esL, sems, XN16)
        P.barrier()
        P.flush(nc, sems)


def attn_l1(B, esL, sems, V):
    nc, P = B.nc, B.P
    X32 = B.consts["X32"]
    ONESC = B.consts["ONESC"]
    ONESR = B.consts["ONESR"]
    ONESR16 = B.consts["ONESR16"]
    with contextlib.ExitStack() as es:
        AO = [B.sbuf(es, f"AO1_{i}", [128, T], BF16) for i in range(NFC)]
        with contextlib.ExitStack() as ep:
            ALBs = B.sbuf(ep, "ALBs", [128, 256], F32)
            P.op("s", lambda e: e.dma_start(out=ALBs[:, :], in_=B.dram["ALB"][:, :]),
                 reads=[B.bufs["ALB"]], writes=[ALBs.b])
            CMs = B.sbuf(ep, "CMs", [128, 4 * TC], BF16)
            P.op("s", lambda e: e.dma_start(out=CMs[:, :], in_=B.dram["CM"][:, :]),
                 reads=[B.bufs["CM"]], writes=[CMs.b])
            QP = [B.sbuf(ep, f"QP1_{i}", [64, T], BF16) for i in range(1)]
            KP = [B.sbuf(ep, f"KP1_{i}", [64, T], BF16) for i in range(1)]
            EB = [B.sbuf(ep, f"EB{i}", [128, 1024], BF16) for i in range(2)]
            ATTS = [B.sbuf(ep, f"ATTS1_{i}", [64, TC], BF16) for i in range(1)]
            RAWH = B.sbuf(ep, "RAWH", [64, T], F32)
            ZHR = B.sbuf(ep, "ZHR", [1, T], F32)
            SQ16 = B.sbuf(ep, "SQ161", [64, TC], BF16)
            ONESC16 = B.sbuf(ep, "ONESC16", [128, 1], BF16)
            P.op("v", lambda e: e.memset(ONESC16[:, :], 1.0), writes=[ONESC16.b])
            HB = B.sbuf(ep, "HBr1", [1, TC], BF16)
            LB = B.sbuf(ep, "LBr1", [1, TC], BF16)
            DF = B.sbuf(ep, "DFr1", [1, TC], F32)
            NRM = B.sbuf(ep, "NRM1", [1, 4], F32)
            DROW = B.sbuf(ep, "DROW", [1, TC], F32)
            D16 = B.sbuf(ep, "D16", [1, T], BF16)
            PS2 = [B.psum(ep, f"PS1_{i}", [128, 1024], F32) for i in range(2)]
            PA = [B.psum(ep, f"PA1h_{i}", [65, TC], F32) for i in range(2)]
            PRW = B.psum(ep, "PRW1", [1, TC], F32)
            PBC = B.psum(ep, "PBC1", [128, TC], F32)

            def emit_combine1(h):
                P.op("c", lambda e: e.activation(ZHR[0:1, :], ZHR[0:1, :], AF.Ln), reads=[ZHR.b], writes=[ZHR.b])
                P.op("c", lambda e: e.activation(ZHR[0:1, :], ZHR[0:1, :], AF.Exp, scale=-1.0),
                     reads=[ZHR.b], writes=[ZHR.b])
                ftile = h // 2
                rbase = 64 * (h % 2)
                for tc in range(NTC):
                    tsl = slice(tc * TC, (tc + 1) * TC)
                    bcast_hl(B, HB, LB, DF, ZHR[0:1, tsl], PBC[0:64, :], 64, [ZHR.b], PBC.b)
                    if rbase == 0:
                        P.op("v", lambda e, ftile=ftile, tsl=tsl: e.tensor_tensor(
                                AO[ftile][0:64, tsl], RAWH[:, tsl], PBC[0:64, :], ALU.mult),
                             reads=[RAWH.b, PBC.b], writes=[AO[ftile].b])
                    else:
                        ats = ATTS[0]
                        P.op("v", lambda e, ats=ats, tsl=tsl: e.tensor_tensor(
                                ats[:, :], RAWH[:, tsl], PBC[0:64, :], ALU.mult),
                             reads=[RAWH.b, PBC.b], writes=[ats.b])
                        P.op("g", lambda e, ftile=ftile, tsl=tsl, ats=ats: e.dma_start(
                                out=AO[ftile][64:128, tsl], in_=ats[:, :]),
                             reads=[ats.b], writes=[AO[ftile].b])

            for h in range(16):
                qp, kp = QP[0], KP[0]
                P.op("g", lambda e, h=h, qp=qp: e.dma_start(out=qp[:, :], in_=B.dram["qTs"][h * 64:(h + 1) * 64, :]),
                     reads=[B.bufs["qTs"]], writes=[qp.b])
                P.op("g", lambda e, h=h, kp=kp: e.dma_start(out=kp[:, :], in_=B.dram["kTs"][h * 64:(h + 1) * 64, :]),
                     reads=[B.bufs["kTs"]], writes=[kp.b])
                for src, col in ((qp, 0), (kp, 1)):
                    for tc in range(NTC):
                        P.op("c", lambda e, src=src, tc=tc: e.activation(
                                SQ16[:, :], src[:, tc * TC:(tc + 1) * TC], AF.Square),
                             reads=[src.b], writes=[SQ16.b])
                        ps_ = PS2[tc // 2]
                        P.op("t", lambda e, tc=tc, ps_=ps_: e.matmul(
                                ps_[0:1, (tc % 2) * TC:((tc % 2) + 1) * TC],
                                ONESC16[0:64, :], SQ16[:, :],
                                start=True, stop=True, skip_group_check=True),
                             reads=[SQ16.b, ONESC16.b], writes=[ps_.b])
                    P.op("v", lambda e: e.reduce_max(NRM[0:1, 2:3], PS2[0][0:1, :], axis=AX.X),
                         reads=[PS2[0].b], writes=[NRM.b])
                    P.op("v", lambda e: e.reduce_max(NRM[0:1, 3:4], PS2[1][0:1, :], axis=AX.X),
                         reads=[PS2[1].b], writes=[NRM.b])
                    P.op("v", lambda e, col=col: e.tensor_max(NRM[0:1, col:col + 1],
                                                              NRM[0:1, 2:3], NRM[0:1, 3:4]),
                         reads=[NRM.b], writes=[NRM.b])
                P.op("v", lambda e: e.tensor_tensor(NRM[0:1, 2:3], NRM[0:1, 0:1], NRM[0:1, 1:2], ALU.mult),
                     reads=[NRM.b], writes=[NRM.b])
                P.op("c", lambda e: e.activation(NRM[0:1, 2:3], NRM[0:1, 2:3], AF.Ln),
                     reads=[NRM.b], writes=[NRM.b])
                P.op("c", lambda e: e.activation(NRM[0:1, 2:3], NRM[0:1, 2:3], AF.Exp, scale=0.5),
                     reads=[NRM.b], writes=[NRM.b])
                for hf in range(4):
                    P.op("s", lambda e, h=h, hf=hf: e.dma_start(
                            out=DROW[0:1, :], in_=B.dram["A8"][h:h + 1, hf * TC:(hf + 1) * TC]),
                         reads=[B.bufs["A8"]], writes=[DROW.b])
                    P.op("v", lambda e: e.tensor_scalar(DROW[0:1, :], DROW[0:1, :], NRM[0:1, 2:3],
                                                        None, ALU.subtract),
                         reads=[DROW.b, NRM.b], writes=[DROW.b])
                    P.op("v", lambda e, hf=hf: e.tensor_copy(D16[0:1, hf * TC:(hf + 1) * TC], DROW[0:1, :]),
                         reads=[DROW.b], writes=[D16.b])
                if h > 0:
                    emit_combine1(h - 1)

                for tcp in range(2):
                    tcA = 2 * tcp
                    tcB = 2 * tcp + 1
                    tAsl = slice(tcA * TC, (tcA + 1) * TC)
                    tBsl = slice(tcB * TC, (tcB + 1) * TC)
                    nsi = 4 * tcB + 4

                    def emit_av1(si, h=h, tcA=tcA, tcB=tcB, nsi=nsi):
                        eb = EB[si % 2]

                        def mkav(e, si=si, h=h, eb=eb):
                            last = None
                            for q, tcx in ((0, tcA), (1, tcB)):
                                if si > 4 * tcx + 3:
                                    continue
                                last = e.matmul(PA[q][0:65, :],
                                                V[si][:, h * 65:(h + 1) * 65],
                                                eb[:, q * TC:(q + 1) * TC],
                                                start=(si == 0), stop=(si == min(4 * tcx + 3, nsi - 1)),
                                                skip_group_check=True)
                            return last
                        P.op("t", mkav, reads=[eb.b, V[si].b], writes=[PA[0].b, PA[1].b])

                    for si in range(nsi):
                        ps_ = PS2[si % 2]
                        eb = EB[si % 2]
                        do_A = si <= 4 * tcA + 3

                        def mksc(e, si=si, kp=kp, qp=qp, tAsl=tAsl, tBsl=tBsl, ps_=ps_, do_A=do_A):
                            last = None
                            for q, tsl, act in ((0, tAsl, do_A), (1, tBsl, True)):
                                if not act:
                                    continue
                                last = e.matmul(ps_[:, q * TC:(q + 1) * TC],
                                                kp[0:64, si * 128:(si + 1) * 128],
                                                qp[0:64, tsl], start=True, stop=False,
                                                skip_group_check=True)
                            for q, tsl, act in ((0, tAsl, do_A), (1, tBsl, True)):
                                if not act:
                                    continue
                                last = e.matmul(ps_[:, q * TC:(q + 1) * TC],
                                                ONESR16[0:1, :], D16[0:1, tsl],
                                                start=False, stop=True, skip_group_check=True)
                            return last
                        P.op("t", mksc, reads=[kp.b, qp.b, D16.b, ONESR16.b], writes=[ps_.b])
                        for q, tcx, act in ((0, tcA, do_A), (1, tcB, True)):
                            if not act:
                                continue
                            if 4 * tcx <= si <= 4 * tcx + 3:
                                mj = si - 4 * tcx
                                P.op("v", lambda e, q=q, mj=mj, ps_=ps_: e.tensor_tensor(
                                        ps_[:, q * TC:(q + 1) * TC], ps_[:, q * TC:(q + 1) * TC],
                                        CMs[:, mj * TC:(mj + 1) * TC], ALU.add),
                                     reads=[CMs.b], writes=[ps_.b])
                            P.op("c", lambda e, q=q, si=si, h=h, eb=eb, ps_=ps_: e.activation(
                                    eb[:, q * TC:(q + 1) * TC], ps_[:, q * TC:(q + 1) * TC], AF.Exp,
                                    bias=ALBs[:, h * 16 + si:h * 16 + si + 1], scale=SC1),
                                 reads=[ps_.b, ALBs.b], writes=[eb.b])
                        if si > 0:
                            emit_av1(si - 1)
                    emit_av1(nsi - 1)
                    for q, tsl in ((0, tAsl), (1, tBsl)):
                        P.op("v", lambda e, tsl=tsl, q=q: e.tensor_copy(RAWH[:, tsl], PA[q][0:64, :]),
                             reads=[PA[q].b], writes=[RAWH.b])
                        P.op("v", lambda e, tsl=tsl, q=q: e.tensor_copy(ZHR[0:1, tsl], PA[q][64:65, :]),
                             reads=[PA[q].b], writes=[ZHR.b])
            emit_combine1(15)
            P.barrier()

        with contextlib.ExitStack() as ew:
            BO = B.sbuf(ew, "BOs", [128, 8], F32)
            P.op("s", lambda e: e.dma_start(out=BO[:, :], in_=B.dram["bo"][:, :]),
                 reads=[B.bufs["bo"]], writes=[BO.b])
            W = load_w_bf16(B, ew, "L1Wo", D, D, "wo1")
            PW = [B.psum(ew, f"PWo1_{i}", [128, 2048], F32) for i in range(2)]
            for fo in range(NFC):
                pw = PW[fo % 2]

                def mkwo(e, fo=fo, pw=pw):
                    last = None
                    for ec in range(NFC):
                        for tc in range(NTC):
                            last = e.matmul(pw[:, tc * TC:(tc + 1) * TC],
                                            W[ec][:, fo * 128:(fo + 1) * 128],
                                            AO[ec][:, tc * TC:(tc + 1) * TC],
                                            start=(ec == 0), stop=(ec == NFC - 1),
                                            skip_group_check=True)
                    return last
                P.op("t", mkwo, reads=[W[ec].b for ec in range(NFC)] + [AO[ec].b for ec in range(NFC)],
                     writes=[pw.b])
                for tc in range(NTC):
                    tsl = slice(tc * TC, (tc + 1) * TC)
                    P.op("v", lambda e, fo=fo, tsl=tsl, pw=pw, tc=tc: e.scalar_tensor_tensor(
                            X32[fo][:, tsl], pw[:, tc * TC:(tc + 1) * TC], BO[:, fo:fo + 1],
                            X32[fo][:, tsl], ALU.add, ALU.add),
                         reads=[pw.b, X32[fo].b, BO.b], writes=[X32[fo].b])
            P.barrier()

def moe_l1(B, esL, sems, XN16_unused):
    nc, P = B.nc, B.P
    ONESR = B.consts["ONESR"]
    with contextlib.ExitStack() as es:
        XN16 = [B.sbuf(es, f"XN16m_{i}", [128, T], BF16) for i in range(NFC)]
        for i in range(NFC):
            P.op("s", lambda e, i=i: e.dma_start(out=XN16[i][:, :],
                                                 in_=B.dram["xn16s"][i * 128:(i + 1) * 128, :]),
                 reads=[B.bufs["xn16s"]], writes=[XN16[i].b])
        HT = [B.sbuf(es, f"HT_{i}", [128, T], BF16) for i in range(32)]
        W1S = [B.sbuf(es, f"W1S_{i}", [128, 1024], BF16) for i in range(2)]
        W2S = [B.sbuf(es, f"W2S_{i}", [128, 4096], BF16) for i in range(2)]
        BCG = B.sbuf(es, "BCG", [128, T], F32)
        TMP = [B.sbuf(es, f"TMPm{i}", [128, T], F32) for i in range(1)]
        GRo = B.sbuf(es, "GRo", [1, 1024], F32)
        HBm = B.sbuf(es, "HBm", [1, TC], BF16)
        LBm = B.sbuf(es, "LBm", [1, TC], BF16)
        DFm = B.sbuf(es, "DFm", [1, TC], F32)
        PH2 = [B.psum(es, f"PH2_{i}", [128, T], F32) for i in range(2)]

        for ei in range(9):
            if ei == 0:
                w1name, w2name = "sw1c", "sw2c"
            else:
                w1name, w2name = f"ew1c{ei - 1}", f"ew2c{ei - 1}"
            if ei > 0:
                for hf in range(2):
                    P.op("s", lambda e, ei=ei, hf=hf: e.dma_start(
                            out=GRo[0:1, :], in_=B.dram["grows"][ei - 1:ei, hf * 1024:(hf + 1) * 1024]),
                         reads=[B.bufs["grows"]], writes=[GRo.b])
                    for q in range(2):
                        tsl = slice((2 * hf + q) * TC, (2 * hf + q + 1) * TC)
                        bcast_hl(B, HBm, LBm, DFm, GRo[0:1, q * TC:(q + 1) * TC], PH2[0][:, 0:TC], 128,
                                 [GRo.b], PH2[0].b)
                        P.op("v", lambda e, tsl=tsl: e.tensor_copy(BCG[:, tsl], PH2[0][:, 0:TC]),
                             reads=[PH2[0].b], writes=[BCG.b])
            w1src = B.dram[w1name]
            for hc in range(32):
                w1s = W1S[hc % 2]
                ph = PH2[hc % 2]
                P.op("s", lambda e, hc=hc, w1s=w1s, w1src=w1src: e.dma_start(
                        out=w1s[:, :],
                        in_=w1src[hc * 128:(hc + 1) * 128, :]),
                     reads=[B.bufs[w1name]], writes=[w1s.b])

                def mkh(e, hc=hc, w1s=w1s, ph=ph):
                    last = None
                    for ec in range(NFC):
                        for tc in range(NTC):
                            last = e.matmul(ph[:, tc * TC:(tc + 1) * TC],
                                            w1s[:, ec * 128:(ec + 1) * 128],
                                            XN16[ec][:, tc * TC:(tc + 1) * TC],
                                            start=(ec == 0), stop=(ec == NFC - 1),
                                            skip_group_check=True)
                    return last
                P.op("t", mkh, reads=[w1s.b] + [XN16[ec].b for ec in range(NFC)], writes=[ph.b])
                P.op("c", lambda e, hc=hc, ph=ph: e.activation(HT[hc][:, :], ph[:, :], AF.Silu),
                     reads=[ph.b], writes=[HT[hc].b])
            w2src = B.dram[w2name]
            for fo in range(NFC):
                w2s = W2S[fo % 2]
                py = PH2[fo % 2]
                P.op("s", lambda e, fo=fo, w2s=w2s, w2src=w2src: e.dma_start(
                        out=w2s[:, :],
                        in_=w2src[fo * 128:(fo + 1) * 128, :]),
                     reads=[B.bufs[w2name]], writes=[w2s.b])

                def mky(e, fo=fo, w2s=w2s, py=py):
                    last = None
                    for hc in range(32):
                        for tc in range(NTC):
                            last = e.matmul(py[:, tc * TC:(tc + 1) * TC],
                                            w2s[:, hc * 128:(hc + 1) * 128],
                                            HT[hc][:, tc * TC:(tc + 1) * TC],
                                            start=(hc == 0), stop=(hc == 31),
                                            skip_group_check=True)
                    return last
                P.op("t", mky, reads=[w2s.b] + [h.b for h in HT], writes=[py.b])
                tmp = TMP[0]
                if ei == 0:
                    P.op("v", lambda e, tmp=tmp, py=py: e.tensor_copy(tmp[:, :], py[:, :]),
                         reads=[py.b], writes=[tmp.b])
                    P.op("g", lambda e, fo=fo, tmp=tmp: e.dma_start(
                            out=B.dram["y32s"][fo * 128:(fo + 1) * 128, :], in_=tmp[:, :]),
                         reads=[tmp.b], writes=[B.bufs["y32s"]])
                else:
                    P.op("v", lambda e, tmp=tmp, py=py: e.tensor_tensor(tmp[:, :], py[:, :], BCG[:, :],
                                                                        ALU.mult),
                         reads=[py.b, BCG.b], writes=[tmp.b])
                    P.op("g", lambda e, fo=fo, tmp=tmp: e.dma_start(
                            out=B.dram["y32s"][fo * 128:(fo + 1) * 128, :], in_=tmp[:, :],
                            accum_op=ALU.add),
                         reads=[tmp.b], writes=[B.bufs["y32s"]])



def _host_arrays(inputs, layers):
    f = np.float32
    bf = ml_dtypes.bfloat16
    A = {}
    if 0 in layers:
        A["L0WqT"] = np.ascontiguousarray(np.asarray(inputs["L0_attn_Wq"]).T.astype(f).astype(bf))
        A["L0WkT"] = np.ascontiguousarray(np.asarray(inputs["L0_attn_Wk"]).T.astype(f).astype(bf))
        A["L0WvT"] = np.ascontiguousarray(np.asarray(inputs["L0_attn_Wv"]).T.astype(f).astype(bf))
        A["L0WoT"] = np.ascontiguousarray(np.asarray(inputs["L0_attn_Wo"]).T.astype(f).astype(bf))
        A["cnw1"] = np.ascontiguousarray(np.asarray(inputs["L0_ffn_cn_w1"]).astype(f).astype(bf))
        A["cnw2"] = np.ascontiguousarray(np.asarray(inputs["L0_ffn_cn_w2"]).astype(f).astype(bf))
        A["cnw3"] = np.ascontiguousarray(np.asarray(inputs["L0_ffn_cn_w3"]).astype(f).astype(bf))
        A["cnw4"] = np.ascontiguousarray(np.asarray(inputs["L0_ffn_cn_w4"]).astype(f).astype(bf))
        A["gcw1"] = np.ascontiguousarray(np.asarray(inputs["L0_ffn_gc_w1"]).astype(f).astype(bf))
        A["gcw2"] = np.ascontiguousarray(np.asarray(inputs["L0_ffn_gc_w2"]).astype(f).astype(bf))
        A["gcw3"] = np.ascontiguousarray(np.asarray(inputs["L0_ffn_gc_w3"]).astype(f).astype(bf))
        A["cnb1"] = np.ascontiguousarray(np.asarray(inputs["L0_ffn_cn_b1"]).astype(f).reshape(8, 128).T)
        A["cnb2"] = np.ascontiguousarray(np.asarray(inputs["L0_ffn_cn_b2"]).astype(f).reshape(4, 128).T)
        A["cnb3"] = np.ascontiguousarray(np.asarray(inputs["L0_ffn_cn_b3"]).astype(f).reshape(2, 128).T)
        A["cnb4"] = np.asarray(inputs["L0_ffn_cn_b4"]).astype(f).reshape(1, 1)
        A["gcb1"] = np.ascontiguousarray(np.asarray(inputs["L0_ffn_gc_b1"]).astype(f).reshape(4, 128).T)
        A["gcb2"] = np.ascontiguousarray(np.asarray(inputs["L0_ffn_gc_b2"]).astype(f).reshape(2, 128).T)
        A["gcb3"] = np.asarray(inputs["L0_ffn_gc_b3"]).astype(f).reshape(1, 1)
        A["arange"] = np.asarray(inputs["L0_ffn_adaptive_range"]).astype(f).reshape(1, 1)
        A["lamq1"] = np.asarray(inputs["L0_attn_lambda_q1"]).astype(f).reshape(1, 32)
        A["lamk1"] = np.asarray(inputs["L0_attn_lambda_k1"]).astype(f).reshape(1, 32)
        A["lamq2"] = np.asarray(inputs["L0_attn_lambda_q2"]).astype(f).reshape(1, 32)
        A["lamk2"] = np.asarray(inputs["L0_attn_lambda_k2"]).astype(f).reshape(1, 32)
        A["subln"] = np.asarray(inputs["L0_attn_subln_w"]).astype(f).reshape(64, 1)
    if 1 in layers:
        A["L1Wq"] = np.ascontiguousarray(np.asarray(inputs["L1_attn_Wq"]).astype(f).astype(bf))
        A["L1Wk"] = np.ascontiguousarray(np.asarray(inputs["L1_attn_Wk"]).astype(f).astype(bf))
        A["L1Wv"] = np.ascontiguousarray(np.asarray(inputs["L1_attn_Wv"]).astype(f).astype(bf))
        A["L1Wo"] = np.ascontiguousarray(np.asarray(inputs["L1_attn_Wo"]).astype(f).astype(bf))
        A["bq"] = np.ascontiguousarray(np.asarray(inputs["L1_attn_bq"]).astype(f).reshape(8, 128).T)
        A["bk"] = np.ascontiguousarray(np.asarray(inputs["L1_attn_bk"]).astype(f).reshape(8, 128).T)
        A["bv"] = np.asarray(inputs["L1_attn_bv"]).astype(f).reshape(1, D)
        A["bo"] = np.ascontiguousarray(np.asarray(inputs["L1_attn_bo"]).astype(f).reshape(8, 128).T)
        def pack_w1(w):
            return np.ascontiguousarray(
                w.reshape(8, 128, 32, 128).transpose(2, 1, 0, 3).reshape(4096, 1024))

        def pack_w2(w):
            return np.ascontiguousarray(
                w.reshape(32, 128, 8, 128).transpose(2, 1, 0, 3).reshape(1024, 4096))

        A["sw1c"] = pack_w1(np.asarray(inputs["L1_ffn_shared_w1"]).T.astype(f).astype(bf))
        A["sw2c"] = pack_w2(np.asarray(inputs["L1_ffn_shared_w2"]).T.astype(f).astype(bf))
        for e in range(8):
            A[f"ew1c{e}"] = pack_w1(np.asarray(inputs["L1_ffn_experts_w1"])[e].T.astype(f).astype(bf))
            A[f"ew2c{e}"] = pack_w2(np.asarray(inputs["L1_ffn_experts_w2"])[e].T.astype(f).astype(bf))
        A["gwT"] = np.ascontiguousarray(np.asarray(inputs["L1_ffn_gate_w"]).T.astype(f))
        slopes = np.array(_get_slopes(16), dtype=np.float64)
        tpos = np.arange(T, dtype=np.float64)
        alibi = -slopes[:, None] * (T - 1 - tpos)[None, :]
        ALBc = np.zeros((128, 256), f)
        for h in range(16):
            for si in range(16):
                ALBc[:, h * 16 + si] = alibi[h, si * 128:(si + 1) * 128]
        A["ALB"] = ALBc
        A["A8"] = (-8.0 * alibi).astype(f)
        CMc = np.zeros((128, 4 * TC), f)
        ds = np.arange(128)[:, None]
        dt = np.arange(TC)[None, :]
        for j in range(4):
            CMc[:, j * TC:(j + 1) * TC] = np.where(dt >= ds + j * 128, 0.0, -2.0e38).astype(f)
        A["CM"] = CMc.astype(ml_dtypes.bfloat16)
        A["IOTA8"] = np.tile(np.arange(8, dtype=f)[None, :], (128, 1))
    A["ID128"] = np.eye(128, dtype=f)
    return A


def kernel(**inputs):
    cfg = dict(CFG)
    layers = tuple(cfg["layers"])
    n_cores = cfg["n_cores"]
    B = build_program(cfg)
    shared = _host_arrays(inputs, layers)
    x = np.asarray(inputs["L0_attn_query"], dtype=np.float32)
    in_maps = []
    for c in range(n_cores):
        m = dict(shared)
        m["xT"] = np.ascontiguousarray(x[c].T)
        in_maps.append(m)
    res = run_bass_kernel_spmd(B.nc, in_maps, core_ids=list(range(n_cores)))
    outs = np.stack([np.ascontiguousarray(res.results[c]["out"].T) for c in range(n_cores)])
    if cfg["debug"]:
        dbg = {nm: [res.results[c][nm] for c in range(n_cores)] for nm in B.debug_outs}
        return outs, dbg
    return outs


# version 24 (final)
# speedup vs baseline: 1.3033x; 1.3033x over previous
import math
import contextlib
import numpy as np
import ml_dtypes
import concourse.bass as bass
import concourse.mybir as mybir
import concourse.bass_utils as _bass_utils
from concourse.bass_utils import run_bass_kernel_spmd


F32 = mybir.dt.float32
F32R = mybir.dt.float32r
BF16 = mybir.dt.bfloat16


def R32(ap):
    return ap.bitcast(F32R)

U32 = mybir.dt.uint32
AF = mybir.ActivationFunctionType
ALU = mybir.AluOpType
AX = mybir.AxisListType

D = 1024
T = 2048
NFC = 8
NTC = 4
TC = 512
NST = 16
LAMBDA_INIT = float(0.8 - 0.6 * np.exp(-0.3 * 12))
SC0 = 1.0 / math.sqrt(32.0)
SC1 = 0.125
EPS = 1e-5

CFG = {
    "layers": (0, 1),
    "debug": (),
    "n_cores": 8,
    "l0_upto": "ffn",
    "l1_upto": "moe",
}


def _get_slopes(n):
    def pow2(n):
        start = 2 ** (-(2 ** (-(math.log2(n) - 3))))
        return [start * start ** i for i in range(n)]
    if math.log2(n).is_integer():
        return pow2(n)
    c = 2 ** math.floor(math.log2(n))
    return pow2(c) + _get_slopes(2 * c)[0::2][:n - c]



class Buf:
    __slots__ = ("wtok", "rtoks", "name")

    def __init__(self, name=""):
        self.wtok = None
        self.rtoks = []
        self.name = name


KEYS = ("s", "g", "t", "v", "c")


class Prog:
    def __init__(self):
        self.items = {k: [] for k in KEYS}
        self.count = {k: 0 for k in KEYS}
        self.seen = {k: {} for k in KEYS}
        self.nblocks = 0

    def op(self, k, fn, reads=(), writes=()):
        deps = []
        for b in reads:
            if b.wtok is not None:
                deps.append(b.wtok)
        for b in writes:
            if b.wtok is not None:
                deps.append(b.wtok)
            deps.extend(b.rtoks)
        it = self.items[k]
        seen = self.seen[k]
        for dk, dv in deps:
            if seen.get(dk, 0) >= dv:
                continue
            it.append(("w", dk, dv))
            seen[dk] = dv
        amt = 16 if k in ("s", "g") else 1
        self.count[k] += amt
        tok = (k, self.count[k])
        it.append(("o", fn, amt))
        for b in writes:
            b.wtok = tok
            b.rtoks = []
        for b in reads:
            if tok not in b.rtoks:
                b.rtoks.append(tok)
        return tok

    def barrier(self):
        for k in KEYS:
            for j in KEYS:
                v = self.count[j]
                if v and self.seen[k].get(j, 0) < v:
                    self.items[k].append(("w", j, v))
                    self.seen[k][j] = v

    def flush(self, nc, sems):
        items = self.items
        self.items = {k: [] for k in KEYS}
        live = [k for k in KEYS if items[k]]
        if not live:
            return
        self.nblocks += 1

        def mkbody(k):
            def body(eng):
                for it in items[k]:
                    if it[0] == "w":
                        eng.wait_ge(sems[it[1]], it[2])
                    else:
                        instr = it[1](eng)
                        instr.then_inc(sems[k], it[2])
            return body

        with nc.Block(f"ph{self.nblocks}") as block:
            if "s" in live:
                block.sync(mkbody("s"))
            if "g" in live:
                block.gpsimd(mkbody("g"))
            if "t" in live:
                block.tensor(mkbody("t"))
            if "v" in live:
                block.vector(mkbody("v"))
            if "c" in live:
                block.scalar(mkbody("c"))


class SB:
    __slots__ = ("t", "b")

    def __init__(self, t, b):
        self.t = t
        self.b = b

    def __getitem__(self, idx):
        return self.t.ap()[idx]

    @property
    def ap(self):
        return self.t.ap()


class Builder:
    def __init__(self, cfg):
        self.cfg = cfg
        self.nc = bass.Bass("TRN2", target_bir_lowering=False)
        self.P = Prog()
        self.dram = {}
        self.debug_outs = []
        self.bufs = {}
        self.consts = {}

    def din(self, name, shape, dt=F32):
        ap = self.nc.dram_tensor(name, list(shape), dt, kind="ExternalInput").ap()
        self.dram[name] = ap
        self.bufs[name] = Buf(name)
        return ap

    def dout(self, name, shape, dt=F32):
        ap = self.nc.dram_tensor(name, list(shape), dt, kind="ExternalOutput").ap()
        self.dram[name] = ap
        self.bufs[name] = Buf(name)
        return ap

    def dscratch(self, name, shape, dt):
        kind = "ExternalOutput" if name in self.cfg["debug"] else "Internal"
        if kind == "ExternalOutput":
            self.debug_outs.append(name)
        ap = self.nc.dram_tensor(name, list(shape), dt, kind=kind).ap()
        self.dram[name] = ap
        self.bufs[name] = Buf(name)
        return ap

    _uid = 0

    def _uniq(self, name):
        Builder._uid += 1
        return f"{name}__{Builder._uid}"

    def sbuf(self, es, name, shape, dt):
        t = es.enter_context(self.nc.sbuf_tensor(self._uniq(name), list(shape), dt))
        return SB(t, Buf(name))

    def psum(self, es, name, shape, dt=F32):
        t = es.enter_context(self.nc.psum_tensor(self._uniq(name), list(shape), dt))
        return SB(t, Buf(name))

    def tap(self, name, src_sb, shape, dt=F32):
        if name not in self.cfg["debug"]:
            return
        nm = "dbg_" + name
        ap = self.dout(nm, shape, dt)
        self.debug_outs.append(nm)
        self.P.op("s", lambda e, ap=ap, s=src_sb: e.dma_start(out=ap[:, :], in_=s.ap[:, :]),
                  reads=[src_sb.b], writes=[self.bufs[nm]])



def build_program(cfg):
    B = Builder(cfg)
    nc, P = B.nc, B.P

    B.din("xT", [D, T])
    B.din("ID128", [128, 128])
    B.dout("out", [D, T])
    if 0 in cfg["layers"]:
        B.din("L0WqT", [D, D], BF16); B.din("L0WkT", [D, D], BF16); B.din("L0WvT", [D, D], BF16); B.din("L0WoT", [D, D], BF16)
        B.din("cnw1", [D, D], BF16); B.din("cnw2", [D, 512], BF16); B.din("cnw3", [512, 256], BF16); B.din("cnw4", [256, 1], BF16)
        B.din("gcw1", [D, 512], BF16); B.din("gcw2", [512, 256], BF16); B.din("gcw3", [256, 1], BF16)
        B.din("cnb1", [128, 8]); B.din("cnb2", [128, 4]); B.din("cnb3", [128, 2]); B.din("cnb4", [1, 1])
        B.din("gcb1", [128, 4]); B.din("gcb2", [128, 2]); B.din("gcb3", [1, 1])
        B.din("lamq1", [1, 32]); B.din("lamk1", [1, 32]); B.din("lamq2", [1, 32]); B.din("lamk2", [1, 32])
        B.din("subln", [64, 1]); B.din("arange", [1, 1])
    if 1 in cfg["layers"]:
        B.din("L1Wq", [D, D], BF16); B.din("L1Wk", [D, D], BF16); B.din("L1Wv", [D, D], BF16); B.din("L1Wo", [D, D], BF16)
        B.din("bq", [128, 8]); B.din("bk", [128, 8]); B.din("bv", [1, D]); B.din("bo", [128, 8])
        B.din("sw1c", [4096, D], BF16); B.din("sw2c", [D, 4096], BF16)
        for e in range(8):
            B.din(f"ew1c{e}", [4096, D], BF16); B.din(f"ew2c{e}", [D, 4096], BF16)
        B.din("gwT", [D, 8])
        B.din("ALB", [128, 256])
        B.din("A8", [16, T])
        B.din("CM", [128, 4 * TC], BF16)
        B.din("IOTA8", [128, 8])
    B.dscratch("xn32s", [D, T], F32)
    B.dscratch("qTs", [D, T], BF16)
    B.dscratch("kTs", [D, T], BF16)
    if 1 in cfg["layers"]:
        B.dscratch("grows", [8, T], F32)
        B.dscratch("xn16s", [D, T], BF16)
        B.dscratch("x32s", [D, T], F32)
        B.dscratch("y32s", [D, T], F32)

    with contextlib.ExitStack() as es0:
        sems = {k: es0.enter_context(nc.semaphore("sem_" + k)) for k in KEYS}

        ONESC = B.sbuf(es0, "ONESC", [128, 1], F32)
        ONESR = B.sbuf(es0, "ONESR", [1, 128], F32)
        ONESR16 = B.sbuf(es0, "ONESR16", [1, 128], BF16)
        EPSA = B.sbuf(es0, "EPSA", [1, 1], F32)
        B.esX32 = contextlib.ExitStack()
        X32 = [B.sbuf(B.esX32, f"X32_{i}", [128, T], F32) for i in range(NFC)]
        B.x32_parked = False
        B.consts = {"X32": X32, "ONESC": ONESC, "ONESR": ONESR,
                    "ONESR16": ONESR16, "EPSA": EPSA}

        P.op("v", lambda e: e.memset(ONESC[:, :], 1.0), writes=[ONESC.b])
        P.op("v", lambda e: e.memset(ONESR[:, :], 1.0), writes=[ONESR.b])
        P.op("v", lambda e: e.memset(ONESR16[:, :], 1.0), writes=[ONESR16.b])
        P.op("v", lambda e: e.memset(EPSA[:, :], EPS), writes=[EPSA.b])
        for i in range(NFC):
            P.op("s", lambda e, i=i: e.dma_start(out=X32[i][:, :], in_=B.dram["xT"][i * 128:(i + 1) * 128, :]),
                 reads=[B.bufs["xT"]], writes=[X32[i].b])
        P.flush(nc, sems)

        if 0 in cfg["layers"]:
            build_layer0(B, sems)
        if 1 in cfg["layers"]:
            build_layer1(B, sems)

        if not B.x32_parked:
            for i in range(NFC):
                P.op("s", lambda e, i=i: e.dma_start(out=B.dram["out"][i * 128:(i + 1) * 128, :],
                                                     in_=X32[i][:, :]),
                     reads=[X32[i].b], writes=[B.bufs["out"]])
        else:
            with contextlib.ExitStack() as ef:
                FA = [B.sbuf(ef, f"FA{i}", [128, T], F32) for i in range(2)]
                FB = [B.sbuf(ef, f"FB{i}", [128, T], F32) for i in range(2)]
                for i in range(NFC):
                    fa, fb = FA[i % 2], FB[i % 2]
                    P.op("s", lambda e, i=i, fa=fa: e.dma_start(
                            out=fa[:, :], in_=B.dram["x32s"][i * 128:(i + 1) * 128, :]),
                         reads=[B.bufs["x32s"]], writes=[fa.b])
                    P.op("g", lambda e, i=i, fb=fb: e.dma_start(
                            out=fb[:, :], in_=B.dram["y32s"][i * 128:(i + 1) * 128, :]),
                         reads=[B.bufs["y32s"]], writes=[fb.b])
                    P.op("v", lambda e, fa=fa, fb=fb: e.tensor_tensor(fa[:, :], fa[:, :], fb[:, :], ALU.add),
                         reads=[fb.b, fa.b], writes=[fa.b])
                    P.op("s", lambda e, i=i, fa=fa: e.dma_start(
                            out=B.dram["out"][i * 128:(i + 1) * 128, :], in_=fa[:, :]),
                         reads=[fa.b], writes=[B.bufs["out"]])
                P.barrier()
        P.items["s"].append(("w", "s", P.count["s"]))
        P.flush(nc, sems)

    return B



def layernorm_rows(B, es_keep, es_scr, tag):
    P = B.P
    X32 = B.consts["X32"]
    ONESC = B.consts["ONESC"]
    EPSA = B.consts["EPSA"]
    M = B.sbuf(es_keep, f"MROW{tag}", [1, T], F32)
    R = B.sbuf(es_keep, f"RROW{tag}", [1, T], F32)
    SQ = [B.sbuf(es_scr, f"LNSQ{tag}_{i}", [128, TC], F32) for i in range(2)]
    PSm = B.psum(es_scr, f"PLNm{tag}", [1, TC], F32)
    PSq = B.psum(es_scr, f"PLNq{tag}", [1, TC], F32)
    for tc in range(NTC):
        sl = slice(tc * TC, (tc + 1) * TC)

        def mk_sum(e, sl=sl):
            last = None
            for i in range(NFC):
                last = e.matmul(PSm[0:1, :], ONESC[:, :], X32[i][:, sl],
                                start=(i == 0), stop=(i == NFC - 1))
            return last
        P.op("t", mk_sum, reads=[X32[i].b for i in range(NFC)] + [ONESC.b], writes=[PSm.b])
        for i in range(NFC):
            sq = SQ[i % 2]
            P.op("c", lambda e, i=i, sl=sl, sq=sq: e.activation(sq[:, :], X32[i][:, sl], AF.Square),
                 reads=[X32[i].b], writes=[sq.b])
            P.op("t", lambda e, i=i, sq=sq: e.matmul(PSq[0:1, :], ONESC[:, :], sq[:, :],
                                                     start=(i == 0), stop=(i == NFC - 1)),
                 reads=[sq.b, ONESC.b], writes=[PSq.b])
        P.op("v", lambda e, sl=sl: e.tensor_scalar(M[0:1, sl], PSm[0:1, :], 1.0 / D, None, ALU.mult),
             reads=[PSm.b], writes=[M.b])
        P.op("v", lambda e, sl=sl: e.tensor_tensor(R[0:1, sl], M[0:1, sl], M[0:1, sl], ALU.mult),
             reads=[M.b], writes=[R.b])
        P.op("v", lambda e, sl=sl: e.scalar_tensor_tensor(R[0:1, sl], PSq[0:1, :], 1.0 / D, R[0:1, sl],
                                                          ALU.mult, ALU.subtract),
             reads=[PSq.b, R.b], writes=[R.b])
        P.op("c", lambda e, sl=sl: e.activation(R[0:1, sl], R[0:1, sl], AF.Ln, bias=EPSA[0:1, :]),
             reads=[R.b, EPSA.b], writes=[R.b])
        P.op("c", lambda e, sl=sl: e.activation(R[0:1, sl], R[0:1, sl], AF.Exp, scale=-0.5),
             reads=[R.b], writes=[R.b])
    return M, R


def ln_apply_bf16(B, es, M, R, XN16, park_name=None, gate_cb=None):
    P = B.P
    X32 = B.consts["X32"]
    ONESR = B.consts["ONESR"]
    PB = B.psum(es, "PLNB", [128, 1024], F32)
    XA = [B.sbuf(es, f"LNXA{i}", [128, TC], F32) for i in range(NFC if gate_cb else 2)]
    HBa = B.sbuf(es, "HBa", [1, TC], BF16)
    LBa = B.sbuf(es, "LBa", [1, TC], BF16)
    DFa = B.sbuf(es, "DFa", [1, TC], F32)
    for tc in range(NTC):
        sl = slice(tc * TC, (tc + 1) * TC)
        bcast_hl(B, HBa, LBa, DFa, M[0:1, sl], PB[:, 0:TC], 128, [M.b], PB.b)
        bcast_hl(B, HBa, LBa, DFa, R[0:1, sl], PB[:, TC:2 * TC], 128, [R.b], PB.b)
        for i in range(NFC):
            xa = XA[i] if gate_cb else XA[i % 2]
            P.op("v", lambda e, i=i, sl=sl, xa=xa: e.tensor_tensor(xa[:, :], X32[i][:, sl], PB[:, 0:TC], ALU.subtract),
                 reads=[X32[i].b, PB.b], writes=[xa.b])
            P.op("v", lambda e, i=i, sl=sl, xa=xa: e.tensor_tensor(xa[:, :], xa[:, :], PB[:, TC:2 * TC], ALU.mult),
                 reads=[xa.b, PB.b], writes=[xa.b])
            P.op("v", lambda e, i=i, sl=sl, xa=xa: e.tensor_copy(XN16[i][:, sl], xa[:, :]),
                 reads=[xa.b], writes=[XN16[i].b])
            if park_name is not None:
                P.op("s", lambda e, i=i, sl=sl, xa=xa: e.dma_start(
                        out=B.dram[park_name][i * 128:(i + 1) * 128, sl], in_=xa[:, :]),
                     reads=[xa.b], writes=[B.bufs[park_name]])
        if gate_cb is not None:
            gate_cb(tc, XA)


def bcast_hl(B, HB, LB, DF, src_ap, out_ap, m, reads, outbuf, scale_ap=None, scale_reads=()):
    P = B.P
    ONESR16 = B.consts["ONESR16"]
    n = src_ap.shape[-1]
    P.op("v", lambda e: e.tensor_copy(DF[0:1, 0:n], src_ap), reads=reads, writes=[DF.b])
    if scale_ap is not None:
        P.op("v", lambda e: e.tensor_scalar(DF[0:1, 0:n], DF[0:1, 0:n], scale_ap, None, ALU.mult),
             reads=[DF.b] + list(scale_reads), writes=[DF.b])
    P.op("v", lambda e: e.tensor_copy(HB[0:1, 0:n], DF[0:1, 0:n]), reads=[DF.b], writes=[HB.b])
    P.op("v", lambda e: e.tensor_tensor(DF[0:1, 0:n], DF[0:1, 0:n], HB[0:1, 0:n], ALU.subtract),
         reads=[DF.b, HB.b], writes=[DF.b])
    P.op("v", lambda e: e.tensor_copy(LB[0:1, 0:n], DF[0:1, 0:n]), reads=[DF.b], writes=[LB.b])

    def mk(e):
        e.matmul(out_ap, ONESR16[0:1, 0:m], HB[0:1, 0:n], start=True, stop=False,
                 skip_group_check=True)
        return e.matmul(out_ap, ONESR16[0:1, 0:m], LB[0:1, 0:n], start=False, stop=True,
                        skip_group_check=True)
    P.op("t", mk, reads=[HB.b, LB.b, ONESR16.b], writes=[outbuf])


def load_w_bf16(B, es, src_name, rows, cols, tag):
    P = B.P
    n = rows // 128
    tiles = [B.sbuf(es, f"W{tag}_{i}", [128, cols], BF16) for i in range(n)]
    for i in range(n):
        P.op("g", lambda e, i=i: e.dma_start(out=tiles[i][:, :],
                                             in_=B.dram[src_name][i * 128:(i + 1) * 128, :]),
             reads=[B.bufs[src_name]], writes=[tiles[i].b])
    return tiles


def proj_qkv(B, es, specs, XN16, V=None, vspec=None):
    P = B.P
    PQ2 = [B.psum(es, f"PQq{i}", [128, 2048], F32) for i in range(2)]
    QS = [B.sbuf(es, f"QSq{i}", [128, T], BF16) for i in range(1)]
    ALLW = {wname: load_w_bf16(B, es, wname, D, D, "pj" + wname)
            for wname, _, _ in specs}
    if vspec is not None:
        ALLW[vspec[0]] = load_w_bf16(B, es, vspec[0], D, D, "pjv")
    for wname, store_name, bias in specs:
        if True:
            W = ALLW[wname]
            for fo in range(NFC):
                PQ = PQ2[fo % 2]

                def mk(e, fo=fo, PQ=PQ, W=W):
                    last = None
                    for ec in range(NFC):
                        for tc in range(NTC):
                            last = e.matmul(PQ[:, tc * TC:(tc + 1) * TC],
                                            W[ec][:, fo * 128:(fo + 1) * 128],
                                            XN16[ec][:, tc * TC:(tc + 1) * TC],
                                            start=(ec == 0), stop=(ec == NFC - 1),
                                            skip_group_check=True)
                    return last
                P.op("t", mk, reads=[W[ec].b for ec in range(NFC)] + [XN16[ec].b for ec in range(NFC)],
                     writes=[PQ.b])
                qs = QS[0]
                if bias is not None:
                    P.op("c", lambda e, qs=qs, fo=fo, PQ=PQ, bias=bias: e.add(qs[:, :], PQ[:, :],
                                                                              bias[:, fo:fo + 1]),
                         reads=[PQ.b, bias.b], writes=[qs.b])
                else:
                    P.op("v", lambda e, qs=qs, PQ=PQ: e.tensor_copy(qs[:, :], PQ[:, :]),
                         reads=[PQ.b], writes=[qs.b])
                P.op("s", lambda e, fo=fo, qs=qs, store_name=store_name: e.dma_start(
                        out=B.dram[store_name][fo * 128:(fo + 1) * 128, :], in_=qs[:, :]),
                     reads=[qs.b], writes=[B.bufs[store_name]])
    if vspec is not None:
        wname, bvb = vspec
        with contextlib.ExitStack() as ew:
            W = ALLW[wname]
            VS = [B.sbuf(ew, f"VSq{i}", [128, 1024], F32) for i in range(1)]
            for st in range(NST):
                PV = PQ2[st % 2]

                def mkv(e, st=st, PV=PV, W=W):
                    last = None
                    for ec in range(NFC):
                        for half in range(2):
                            last = e.matmul(PV[:, half * 512:(half + 1) * 512],
                                            XN16[ec][:, st * 128:(st + 1) * 128],
                                            W[ec][:, half * 512:(half + 1) * 512],
                                            start=(ec == 0), stop=(ec == NFC - 1),
                                            skip_group_check=True)
                    return last
                P.op("t", mkv, reads=[W[ec].b for ec in range(NFC)] + [XN16[ec].b for ec in range(NFC)],
                     writes=[PV.b])
                vs = VS[0]
                if bvb is not None:
                    P.op("v", lambda e, vs=vs, PV=PV: e.tensor_tensor(vs[:, :], PV[:, 0:1024], bvb[:, :], ALU.add),
                         reads=[PV.b, bvb.b], writes=[vs.b])
                else:
                    P.op("v", lambda e, vs=vs, PV=PV: e.tensor_copy(vs[:, :], PV[:, 0:1024]),
                         reads=[PV.b], writes=[vs.b])
                dstap = V[st].ap.rearrange("p (g c) -> p g c", g=16)[:, :, 0:64]
                srcap = vs.ap.rearrange("p (g c) -> p g c", g=16)
                P.op("v", lambda e, dstap=dstap, srcap=srcap: e.tensor_copy(dstap, srcap),
                     reads=[vs.b], writes=[V[st].b])


def proj_featmajor(B, es, W, XN16, store_name, bias=None, scale=None):
    P = B.P
    PQ2 = [B.psum(es, f"PQp{i}", [128, 2048], F32) for i in range(2)]
    QS = [B.sbuf(es, f"QSp{i}", [128, T], BF16) for i in range(2)]
    for fo in range(NFC):
        PQ = PQ2[fo % 2]

        def mk(e, fo=fo, PQ=PQ):
            last = None
            for ec in range(NFC):
                for tc in range(NTC):
                    last = e.matmul(PQ[:, tc * TC:(tc + 1) * TC],
                                    W[ec][:, fo * 128:(fo + 1) * 128],
                                    XN16[ec][:, tc * TC:(tc + 1) * TC],
                                    start=(ec == 0), stop=(ec == NFC - 1),
                                    skip_group_check=True)
            return last
        P.op("t", mk, reads=[W[ec].b for ec in range(NFC)] + [XN16[ec].b for ec in range(NFC)],
             writes=[PQ.b])
        qs = QS[fo % 2]
        if bias is not None:
            P.op("c", lambda e, qs=qs, fo=fo, PQ=PQ: e.add(qs[:, :], PQ[:, :], bias[:, fo:fo + 1]),
                 reads=[PQ.b, bias.b], writes=[qs.b])
        elif scale is not None:
            P.op("c", lambda e, qs=qs, PQ=PQ: e.mul(qs[:, :], PQ[:, :], scale),
                 reads=[PQ.b], writes=[qs.b])
        else:
            P.op("v", lambda e, qs=qs, PQ=PQ: e.tensor_copy(qs[:, :], PQ[:, :]),
                 reads=[PQ.b], writes=[qs.b])
        P.op("s", lambda e, fo=fo, qs=qs: e.dma_start(
                out=B.dram[store_name][fo * 128:(fo + 1) * 128, :], in_=qs[:, :]),
             reads=[qs.b], writes=[B.bufs[store_name]])


def proj_v_tokmajor(B, es, W, XN16, V, ngroups, bvb=None):
    P = B.P
    for st in range(NST):
        dstap = V[st].ap.rearrange("p (g c) -> p g c", g=ngroups)[:, :, 64:65]
        P.op("v", lambda e, dstap=dstap: e.memset(dstap, 1.0), writes=[V[st].b])
    PV2 = [B.psum(es, f"PVp{i}", [128, 1024], F32) for i in range(2)]
    VS = [B.sbuf(es, f"VSp{i}", [128, 1024], F32) for i in range(2)]
    for st in range(NST):
        PV = PV2[st % 2]

        def mkv(e, st=st, PV=PV):
            last = None
            for ec in range(NFC):
                for half in range(2):
                    last = e.matmul(PV[:, half * 512:(half + 1) * 512],
                                    XN16[ec][:, st * 128:(st + 1) * 128],
                                    W[ec][:, half * 512:(half + 1) * 512],
                                    start=(ec == 0), stop=(ec == NFC - 1),
                                    skip_group_check=True)
            return last
        P.op("t", mkv, reads=[W[ec].b for ec in range(NFC)] + [XN16[ec].b for ec in range(NFC)],
             writes=[PV.b])
        vs = VS[st % 2]
        if bvb is not None:
            P.op("v", lambda e, vs=vs, PV=PV: e.tensor_tensor(vs[:, :], PV[:, :], bvb[:, :], ALU.add),
                 reads=[PV.b, bvb.b], writes=[vs.b])
        else:
            P.op("v", lambda e, vs=vs, PV=PV: e.tensor_copy(vs[:, :], PV[:, :]), reads=[PV.b], writes=[vs.b])
        dstap = V[st].ap.rearrange("p (g c) -> p g c", g=16)[:, :, 0:64]
        srcap = vs.ap.rearrange("p (g c) -> p g c", g=16)
        P.op("v", lambda e, dstap=dstap, srcap=srcap: e.tensor_copy(dstap, srcap),
             reads=[vs.b], writes=[V[st].b])



def build_layer0(B, sems):
    nc, P = B.nc, B.P
    with contextlib.ExitStack() as esL:
        esV = contextlib.ExitStack()
        V = [B.sbuf(esV, f"V0_{st}", [128, 16 * 65], BF16) for st in range(NST)]
        esMR = contextlib.ExitStack()
        with contextlib.ExitStack() as es:
            M, R = layernorm_rows(B, esMR, es, "0")
            P.barrier()
        P.flush(nc, sems)

        with contextlib.ExitStack() as esx:
            XN16 = [B.sbuf(esx, f"XN16_{i}", [128, T], BF16) for i in range(NFC)]
            with contextlib.ExitStack() as es:
                ln_apply_bf16(B, es, M, R, XN16, park_name="xn32s")
                P.barrier()
            P.flush(nc, sems)
            B.tap("xn16_0", XN16[0], [128, T], BF16)
            if B.cfg.get("l0_upto") != "ln":
                for st in range(NST):
                    dstap = V[st].ap.rearrange("p (g c) -> p g c", g=16)[:, :, 64:65]
                    P.op("v", lambda e, dstap=dstap: e.memset(dstap, 1.0), writes=[V[st].b])
                with contextlib.ExitStack() as es:
                    proj_qkv(B, es, [("L0WqT", "qTs", None), ("L0WkT", "kTs", None)],
                             XN16, V=V, vspec=("L0WvT", None))
                    P.barrier()
                P.flush(nc, sems)
        if B.cfg.get("l0_upto") in ("ln", "qkv"):
            esMR.close()
            esV.close()
            return

        esMR.close()
        attn_l0(B, esL, sems, V)
        P.barrier()
        P.flush(nc, sems)
        esV.close()
        if B.cfg.get("l0_upto") == "attn":
            return

        ffn_l0(B, esL, sems)
        P.barrier()
        P.flush(nc, sems)


def attn_l0(B, esL, sems, V):
    nc, P = B.nc, B.P
    X32 = B.consts["X32"]
    ONESC = B.consts["ONESC"]
    ONESR = B.consts["ONESR"]
    EPSA = B.consts["EPSA"]
    with contextlib.ExitStack() as es:
        LAM = B.sbuf(es, "LAM", [1, 2], F32)
        Lr = [B.sbuf(es, f"Lr{i}", [1, 32], F32) for i in range(4)]
        for i, nm in enumerate(("lamq1", "lamk1", "lamq2", "lamk2")):
            P.op("s", lambda e, nm=nm, i=i: e.dma_start(out=Lr[i][:, :], in_=B.dram[nm][:, :]),
                 reads=[B.bufs[nm]], writes=[Lr[i].b])
        P.op("v", lambda e: e.tensor_mul(Lr[0][:, :], Lr[0][:, :], Lr[1][:, :]), reads=[Lr[0].b, Lr[1].b], writes=[Lr[0].b])
        P.op("v", lambda e: e.tensor_mul(Lr[2][:, :], Lr[2][:, :], Lr[3][:, :]), reads=[Lr[2].b, Lr[3].b], writes=[Lr[2].b])
        P.op("v", lambda e: e.reduce_sum(LAM[0:1, 0:1], Lr[0][:, :], axis=AX.X), reads=[Lr[0].b], writes=[LAM.b])
        P.op("v", lambda e: e.reduce_sum(LAM[0:1, 1:2], Lr[2][:, :], axis=AX.X), reads=[Lr[2].b], writes=[LAM.b])
        P.op("c", lambda e: e.activation(LAM[:, :], LAM[:, :], AF.Exp), reads=[LAM.b], writes=[LAM.b])
        P.op("v", lambda e: e.tensor_sub(LAM[0:1, 0:1], LAM[0:1, 0:1], LAM[0:1, 1:2]), reads=[LAM.b], writes=[LAM.b])
        P.op("v", lambda e: e.tensor_scalar(LAM[0:1, 0:1], LAM[0:1, 0:1], LAMBDA_INIT, None, ALU.add),
             reads=[LAM.b], writes=[LAM.b])

        SUBL = B.sbuf(es, "SUBL", [64, 1], F32)
        P.op("s", lambda e: e.dma_start(out=SUBL[:, :], in_=B.dram["subln"][:, :]),
             reads=[B.bufs["subln"]], writes=[SUBL.b])
        P.op("v", lambda e: e.tensor_scalar(SUBL[:, :], SUBL[:, :], 1.0 - LAMBDA_INIT, None, ALU.mult),
             reads=[SUBL.b], writes=[SUBL.b])

        AO = [B.sbuf(es, f"AO{i}", [128, T], BF16) for i in range(NFC)]

        with contextlib.ExitStack() as ep:
            QP = [B.sbuf(ep, f"QP{i}", [64, T], BF16) for i in range(2)]
            KP = [B.sbuf(ep, f"KP{i}", [64, T], BF16) for i in range(2)]
            E0 = [B.sbuf(ep, f"E0{i}", [128, 1024], BF16) for i in range(2)]
            E1 = E0
            ATTS = [B.sbuf(ep, f"ATTS{i}", [64, TC], BF16) for i in range(1)]
            SEG2 = B.sbuf(ep, "SEG2", [64, 2], BF16)
            P.op("v", lambda e: e.memset(SEG2[:, :], 0.0), writes=[SEG2.b])
            P.op("v", lambda e: e.memset(SEG2[0:32, 0:1], 1.0), writes=[SEG2.b])
            P.op("v", lambda e: e.memset(SEG2[32:64, 1:2], 1.0), writes=[SEG2.b])
            IDs = B.sbuf(ep, "IDs", [128, 128], F32)
            P.op("s", lambda e: e.dma_start(out=IDs[:, :], in_=B.dram["ID128"][:, :]),
                 reads=[B.bufs["ID128"]], writes=[IDs.b])
            QN2 = B.sbuf(ep, "QN2", [2, 4], F32)
            HB = B.sbuf(ep, "HBr", [1, TC], BF16)
            LB = B.sbuf(ep, "LBr", [1, TC], BF16)
            DF = B.sbuf(ep, "DFr", [1, TC], F32)
            RAW0s = [B.sbuf(ep, f"RAW0_{i}", [64, T], F32) for i in range(2)]
            RAW1s = [B.sbuf(ep, f"RAW1_{i}", [64, T], F32) for i in range(2)]
            ZALL = B.sbuf(ep, "ZALL", [128, T], F32)
            PZ = (0, 32, 64, 96)
            ZB = [Buf(f"ZB{r}") for r in range(4)]
            RR = B.sbuf(ep, "RRrow", [1, T], F32)
            SQA = B.sbuf(ep, "SQA", [64, TC], F32)
            SQ16 = [B.sbuf(ep, f"SQ16_{i}", [64, TC], BF16) for i in range(2)]
            NRM = B.sbuf(ep, "NRM", [1, 8], F32)
            NEGCC = B.sbuf(ep, "NEGCC", [128, 2], F32)
            PS2 = [B.psum(ep, f"PS0_{i}", [128, 1024], F32) for i in range(2)]
            PA0 = [B.psum(ep, f"PA0_{i}", [65, TC], F32) for i in range(2)]
            PA1 = PA0
            PRW = B.psum(ep, "PRW", [1, TC], F32)
            PBC = B.psum(ep, "PBC", [128, TC], F32)

            def combine_stages(p):
                RAW0, RAW1 = RAW0s[p % 2], RAW1s[p % 2]
                z0, z1 = 2 * (p % 2), 2 * (p % 2) + 1

                def zmul(tc):
                    tsl = slice(tc * TC, (tc + 1) * TC)
                    for (zr, raw) in ((z0, RAW0), (z1, RAW1)):
                        zp = PZ[zr]
                        sc = LAM[0:1, 0:1] if zr == z1 else None
                        bcast_hl(B, HB, LB, DF, ZALL[zp:zp + 1, tsl], PBC[0:64, :], 64,
                                 [ZB[zr]], PBC.b, scale_ap=sc,
                                 scale_reads=[LAM.b] if sc is not None else ())
                        P.op("v", lambda e, tsl=tsl, raw=raw: e.tensor_tensor(
                                raw[:, tsl], raw[:, tsl], PBC[0:64, :], ALU.mult),
                             reads=[PBC.b, raw.b], writes=[raw.b])

                def s1():
                    for zr in (z0, z1):
                        zp = PZ[zr]
                        P.op("c", lambda e, zp=zp: e.activation(ZALL[zp:zp + 1, :], ZALL[zp:zp + 1, :], AF.Ln),
                             reads=[ZB[zr]], writes=[ZB[zr]])
                        P.op("c", lambda e, zp=zp: e.activation(ZALL[zp:zp + 1, :], ZALL[zp:zp + 1, :], AF.Exp,
                                                                scale=-1.0),
                             reads=[ZB[zr]], writes=[ZB[zr]])
                    zmul(0)
                    zmul(1)

                def s2():
                    zmul(2)
                    zmul(3)
                    P.op("v", lambda e: e.tensor_sub(RAW0[:, :], RAW0[:, :], RAW1[:, :]),
                         reads=[RAW0.b, RAW1.b], writes=[RAW0.b])

                def s3():
                    for tc in range(NTC):
                        tsl = slice(tc * TC, (tc + 1) * TC)
                        P.op("c", lambda e, tsl=tsl: e.activation(SQA[:, :], RAW0[:, tsl], AF.Square),
                             reads=[RAW0.b], writes=[SQA.b])
                        P.op("t", lambda e: e.matmul(PRW[0:1, :], ONESC[0:64, :], SQA[:, :],
                                                     start=True, stop=True),
                             reads=[SQA.b, ONESC.b], writes=[PRW.b])
                        P.op("c", lambda e, tsl=tsl: e.activation(RR[0:1, tsl], PRW[0:1, :], AF.Ln,
                                                                  bias=EPSA[0:1, :], scale=1.0 / 64),
                             reads=[PRW.b, EPSA.b], writes=[RR.b])
                    P.op("c", lambda e: e.activation(RR[0:1, :], RR[0:1, :], AF.Exp, scale=-0.5),
                         reads=[RR.b], writes=[RR.b])

                def s4():
                    ftile = p // 2
                    rbase = 64 * (p % 2)
                    for tc in range(NTC):
                        tsl = slice(tc * TC, (tc + 1) * TC)
                        bcast_hl(B, HB, LB, DF, RR[0:1, tsl], PBC[0:64, :], 64, [RR.b], PBC.b)
                        P.op("v", lambda e, tsl=tsl: e.tensor_tensor(RAW0[:, tsl], RAW0[:, tsl],
                                                                     PBC[0:64, :], ALU.mult),
                             reads=[PBC.b, RAW0.b], writes=[RAW0.b])
                        if rbase == 0:
                            P.op("v", lambda e, ftile=ftile, tsl=tsl: e.tensor_scalar(
                                    AO[ftile][0:64, tsl], RAW0[:, tsl], SUBL[:, :], None, ALU.mult),
                                 reads=[RAW0.b, SUBL.b], writes=[AO[ftile].b])
                        else:
                            ats = ATTS[0]
                            P.op("v", lambda e, ats=ats, tsl=tsl: e.tensor_scalar(
                                    ats[:, :], RAW0[:, tsl], SUBL[:, :], None, ALU.mult),
                                 reads=[RAW0.b, SUBL.b], writes=[ats.b])
                            P.op("g", lambda e, ftile=ftile, tsl=tsl, ats=ats: e.dma_start(
                                    out=AO[ftile][64:128, tsl], in_=ats[:, :]),
                                 reads=[ats.b], writes=[AO[ftile].b])

                return [s1, s2, s3, s4]

            for p in range(16):
                qp, kp = QP[p % 2], KP[p % 2]
                if p == 0:
                    P.op("g", lambda e, qp=qp: e.dma_start(out=qp[:, :], in_=B.dram["qTs"][0:64, :]),
                         reads=[B.bufs["qTs"]], writes=[qp.b])
                    P.op("g", lambda e, kp=kp: e.dma_start(out=kp[:, :], in_=B.dram["kTs"][0:64, :]),
                         reads=[B.bufs["kTs"]], writes=[kp.b])
                for src, o in ((qp, 0), (kp, 1)):
                    for tc in range(NTC):
                        sq = SQ16[tc % 2]
                        P.op("c", lambda e, src=src, tc=tc, sq=sq: e.activation(
                                sq[:, :], src[:, tc * TC:(tc + 1) * TC], AF.Square),
                             reads=[src.b], writes=[sq.b])
                        ps_ = PS2[tc // 2]
                        P.op("t", lambda e, tc=tc, ps_=ps_, sq=sq: e.matmul(
                                ps_[0:2, (tc % 2) * TC:((tc % 2) + 1) * TC],
                                SEG2[:, :], sq[:, :],
                                start=True, stop=True, skip_group_check=True),
                             reads=[sq.b, SEG2.b], writes=[ps_.b])
                    P.op("v", lambda e, o=o: e.reduce_max(QN2[0:2, 2:3], PS2[0][0:2, :], axis=AX.X),
                         reads=[PS2[0].b], writes=[QN2.b])
                    P.op("v", lambda e, o=o: e.reduce_max(QN2[0:2, 3:4], PS2[1][0:2, :], axis=AX.X),
                         reads=[PS2[1].b], writes=[QN2.b])
                    P.op("v", lambda e, o=o: e.tensor_max(QN2[0:2, o:o + 1], QN2[0:2, 2:3], QN2[0:2, 3:4]),
                         reads=[QN2.b], writes=[QN2.b])
                P.op("v", lambda e: e.tensor_tensor(QN2[0:2, 2:3], QN2[0:2, 0:1], QN2[0:2, 1:2], ALU.mult),
                     reads=[QN2.b], writes=[QN2.b])
                P.op("t", lambda e: e.transpose(PRW[0:1, 0:2], QN2[0:2, 2:3], IDs[0:2, 0:2]),
                     reads=[QN2.b, IDs.b], writes=[PRW.b])
                P.op("v", lambda e: e.tensor_copy(NRM[0:1, 4:6], PRW[0:1, 0:2]),
                     reads=[PRW.b], writes=[NRM.b])
                P.op("c", lambda e: e.activation(NRM[0:1, 4:6], NRM[0:1, 4:6], AF.Ln), reads=[NRM.b], writes=[NRM.b])
                P.op("c", lambda e: e.activation(NRM[0:1, 4:6], NRM[0:1, 4:6], AF.Exp, scale=0.5),
                     reads=[NRM.b], writes=[NRM.b])
                P.op("v", lambda e: e.tensor_scalar(NRM[0:1, 4:6], NRM[0:1, 4:6], -SC0, None, ALU.mult),
                     reads=[NRM.b], writes=[NRM.b])
                for j in range(2):
                    P.op("t", lambda e, j=j: e.matmul(PBC[:, 0:1], ONESR[:, :], NRM[0:1, 4 + j:5 + j],
                                                      start=True, stop=True),
                         reads=[NRM.b, ONESR.b], writes=[PBC.b])
                    P.op("v", lambda e, j=j: e.tensor_copy(NEGCC[:, j:j + 1], PBC[:, 0:1]),
                         reads=[PBC.b], writes=[NEGCC.b])
                if p < 15:
                    qn, kn = QP[(p + 1) % 2], KP[(p + 1) % 2]
                    P.op("g", lambda e, p=p, qn=qn: e.dma_start(
                            out=qn[:, :], in_=B.dram["qTs"][(p + 1) * 64:(p + 2) * 64, :]),
                         reads=[B.bufs["qTs"]], writes=[qn.b])
                    P.op("g", lambda e, p=p, kn=kn: e.dma_start(
                            out=kn[:, :], in_=B.dram["kTs"][(p + 1) * 64:(p + 2) * 64, :]),
                         reads=[B.bufs["kTs"]], writes=[kn.b])
                stages = combine_stages(p - 1) if p > 0 else []
                bi = 0

                for tcp in range(2):
                    t0sl = slice((2 * tcp) * TC, (2 * tcp + 1) * TC)
                    t1sl = slice((2 * tcp + 1) * TC, (2 * tcp + 2) * TC)
                    for hj in range(2):
                        base = 32 * hj
                        ebufs = E0 if hj == 0 else E1
                        pa = PA0 if hj == 0 else PA1

                        def emit_av0(si, pa=pa, p=p, ebufs=ebufs):
                            eb = ebufs[si % 2]

                            def mkav(e, si=si, p=p, eb=eb, pa=pa):
                                last = None
                                for q in range(2):
                                    last = e.matmul(pa[q][0:65, :],
                                                    V[si][:, p * 65:(p + 1) * 65],
                                                    eb[:, q * TC:(q + 1) * TC],
                                                    start=(si == 0), stop=(si == 15),
                                                    skip_group_check=True)
                                return last
                            P.op("t", mkav, reads=[eb.b, V[si].b], writes=[pa[0].b, pa[1].b])

                        for si in range(16):
                            ps_ = PS2[si % 2]
                            ebuf = ebufs[si % 2]

                            def mksc(e, si=si, base=base, kp=kp, qp=qp, t0sl=t0sl, t1sl=t1sl, ps_=ps_):
                                e.matmul(ps_[:, 0:TC],
                                         kp[base:base + 32, si * 128:(si + 1) * 128],
                                         qp[base:base + 32, t0sl],
                                         start=True, stop=True, tile_position=(base, 0),
                                         skip_group_check=True)
                                return e.matmul(ps_[:, TC:2 * TC],
                                                kp[base:base + 32, si * 128:(si + 1) * 128],
                                                qp[base:base + 32, t1sl],
                                                start=True, stop=True, tile_position=(base, 0),
                                                skip_group_check=True)
                            P.op("t", mksc, reads=[kp.b, qp.b], writes=[ps_.b])
                            P.op("c", lambda e, hj=hj, ebuf=ebuf, ps_=ps_: e.activation(
                                    ebuf[:, :], ps_[:, :], AF.Exp, bias=NEGCC[:, hj:hj + 1], scale=SC0),
                                 reads=[ps_.b, NEGCC.b], writes=[ebuf.b])
                            if si > 0:
                                emit_av0(si - 1)
                        emit_av0(15)
                        raw = (RAW0s if hj == 0 else RAW1s)[p % 2]
                        zr = 2 * (p % 2) + hj
                        zp = PZ[zr]
                        for q, tsl in ((0, t0sl), (1, t1sl)):
                            P.op("v", lambda e, tsl=tsl, q=q, raw=raw, pa=pa: e.tensor_copy(
                                    raw[:, tsl], pa[q][0:64, :]),
                                 reads=[pa[q].b], writes=[raw.b])
                            P.op("v", lambda e, tsl=tsl, q=q, zp=zp, pa=pa: e.tensor_copy(
                                    ZALL[zp:zp + 1, tsl], pa[q][64:65, :]),
                                 reads=[pa[q].b], writes=[ZB[zr]])
                        if bi < len(stages):
                            stages[bi]()
                            bi += 1
            for st in combine_stages(15):
                st()
            P.barrier()

        with contextlib.ExitStack() as ew:
            W = load_w_bf16(B, ew, "L0WoT", D, D, "wo")
            PW = [B.psum(ew, f"PWo{i}", [128, 2048], F32) for i in range(2)]
            for fo in range(NFC):
                pw = PW[fo % 2]

                def mkwo(e, fo=fo, pw=pw):
                    last = None
                    for ec in range(NFC):
                        for tc in range(NTC):
                            last = e.matmul(pw[:, tc * TC:(tc + 1) * TC],
                                            W[ec][:, fo * 128:(fo + 1) * 128],
                                            AO[ec][:, tc * TC:(tc + 1) * TC],
                                            start=(ec == 0), stop=(ec == NFC - 1),
                                            skip_group_check=True)
                    return last
                P.op("t", mkwo, reads=[W[ec].b for ec in range(NFC)] + [AO[ec].b for ec in range(NFC)],
                     writes=[pw.b])
                for tc in range(NTC):
                    tsl = slice(tc * TC, (tc + 1) * TC)
                    P.op("v", lambda e, fo=fo, tsl=tsl, pw=pw, tc=tc: e.tensor_tensor(
                            X32[fo][:, tsl], X32[fo][:, tsl], pw[:, tc * TC:(tc + 1) * TC], ALU.add),
                         reads=[pw.b, X32[fo].b], writes=[X32[fo].b])
            P.barrier()

def ffn_l0(B, esL, sems):
    nc, P = B.nc, B.P
    X32 = B.consts["X32"]
    ONESR = B.consts["ONESR"]
    with contextlib.ExitStack() as es:
        OUT32 = [B.sbuf(es, f"OUT32_{i}", [128, T], F32) for i in range(NFC)]
        OB = {(i, tc): Buf(f"OB{i}_{tc}") for i in range(NFC) for tc in range(NTC)}
        O16 = [B.sbuf(es, f"O16_{i}", [128, TC], BF16) for i in range(NFC)]
        for i in range(NFC):
            P.op("s", lambda e, i=i: e.dma_start(out=OUT32[i][:, :], in_=B.dram["xn32s"][i * 128:(i + 1) * 128, :]),
                 reads=[B.bufs["xn32s"]], writes=[OB[(i, tc)] for tc in range(NTC)])
        CW1 = load_w_bf16(B, es, "cnw1", D, D, "c1")
        CW2 = load_w_bf16(B, es, "cnw2", D, 512, "c2")
        CW3 = load_w_bf16(B, es, "cnw3", 512, 256, "c3")
        CW4 = load_w_bf16(B, es, "cnw4", 256, 1, "c4")
        BIAS = {}
        for nm, shape in (("cnb1", [128, 8]), ("cnb2", [128, 4]), ("cnb3", [128, 2]), ("cnb4", [1, 1]),
                          ("gcb1", [128, 4]), ("gcb2", [128, 2]), ("gcb3", [1, 1]), ("arange", [1, 1])):
            t = B.sbuf(es, "sb_" + nm, shape, F32)
            P.op("s", lambda e, nm=nm, t=t: e.dma_start(out=t[:, :], in_=B.dram[nm][:, :]),
                 reads=[B.bufs[nm]], writes=[t.b])
            BIAS[nm] = t
        CC = B.sbuf(es, "CC", [1, 2], F32)
        P.op("v", lambda e: e.tensor_scalar(CC[0:1, 0:1], BIAS["arange"][0:1, :], 2.0, None, ALU.mult),
             reads=[BIAS["arange"].b], writes=[CC.b])
        P.op("v", lambda e: e.tensor_scalar(CC[0:1, 1:2], BIAS["arange"][0:1, :], -1.0, 0.1, ALU.mult, ALU.add),
             reads=[BIAS["arange"].b], writes=[CC.b])

        GF = B.sbuf(es, "GF", [1, 2], F32)
        H1 = [B.sbuf(es, f"H1_{i}", [128, TC], BF16) for i in range(NFC)]
        H2 = [B.sbuf(es, f"H2_{i}", [128, TC], BF16) for i in range(4)]
        H3 = [B.sbuf(es, f"H3_{i}", [128, TC], BF16) for i in range(2)]
        TF = B.sbuf(es, "TFROW", [1, TC], F32)
        HBf = B.sbuf(es, "HBf", [1, TC], BF16)
        LBf = B.sbuf(es, "LBf", [1, TC], BF16)
        DFf = B.sbuf(es, "DFf", [1, TC], F32)
        TW = [B.sbuf(es, f"TW{i}", [128, TC], F32) for i in range(4)]
        PH = [B.psum(es, f"PH{i}", [128, TC], F32) for i in range(2)]
        PCP2 = [B.psum(es, f"PCP{i}", [128, TC], F32) for i in range(2)]
        PR2 = B.psum(es, "PR2", [1, TC], F32)

        esG = contextlib.ExitStack()
        GW1 = load_w_bf16(B, esG, "gcw1", D, 512, "g1")
        GW2 = load_w_bf16(B, esG, "gcw2", 512, 256, "g2")
        GW3 = load_w_bf16(B, esG, "gcw3", 256, 1, "g3")
        GC = B.sbuf(esG, "GC", [128, 8], F32)
        GC16 = B.sbuf(esG, "GC16", [128, 8], BF16)
        HG1 = B.sbuf(esG, "HG1", [128, 4], BF16)
        HG2 = B.sbuf(esG, "HG2", [128, 2], BF16)
        PG = B.psum(esG, "PG", [128, 8], F32)
        for i in range(NFC):
            P.op("v", lambda e, i=i: e.reduce_sum(GC[:, i:i + 1], OUT32[i][:, :], axis=AX.X),
                 reads=[OB[(i, tc)] for tc in range(NTC)], writes=[GC.b])
        P.op("v", lambda e: e.tensor_scalar(GC[:, :], GC[:, :], 1.0 / T, None, ALU.mult),
             reads=[GC.b], writes=[GC.b])
        P.op("v", lambda e: e.tensor_copy(GC16[:, :], GC[:, :]), reads=[GC.b], writes=[GC16.b])

        def mkg1(e):
            last = None
            for fo in range(4):
                for ec in range(NFC):
                    last = e.matmul(PG[:, fo:fo + 1], GW1[ec][:, fo * 128:(fo + 1) * 128],
                                    GC16[:, ec:ec + 1], start=(ec == 0), stop=(ec == NFC - 1),
                                    skip_group_check=True)
            return last
        P.op("t", mkg1, reads=[GC16.b] + [w.b for w in GW1], writes=[PG.b])
        for fo in range(4):
            P.op("c", lambda e, fo=fo: e.activation(HG1[:, fo:fo + 1], PG[:, fo:fo + 1], AF.Gelu,
                                                    bias=BIAS["gcb1"][:, fo:fo + 1]),
                 reads=[PG.b, BIAS["gcb1"].b], writes=[HG1.b])

        def mkg2(e):
            last = None
            for fo in range(2):
                for ec in range(4):
                    last = e.matmul(PG[:, 4 + fo:5 + fo], GW2[ec][:, fo * 128:(fo + 1) * 128],
                                    HG1[:, ec:ec + 1], start=(ec == 0), stop=(ec == 3),
                                    skip_group_check=True)
            return last
        P.op("t", mkg2, reads=[HG1.b] + [w.b for w in GW2], writes=[PG.b])
        for fo in range(2):
            P.op("c", lambda e, fo=fo: e.activation(HG2[:, fo:fo + 1], PG[:, 4 + fo:5 + fo], AF.Gelu,
                                                    bias=BIAS["gcb2"][:, fo:fo + 1]),
                 reads=[PG.b, BIAS["gcb2"].b], writes=[HG2.b])

        def mkg3(e):
            last = None
            for ec in range(2):
                last = e.matmul(PG[0:1, 6:7], GW3[ec][:, 0:1], HG2[:, ec:ec + 1],
                                start=(ec == 0), stop=(ec == 1), skip_group_check=True)
            return last
        P.op("t", mkg3, reads=[HG2.b] + [w.b for w in GW3], writes=[PG.b])
        P.op("c", lambda e: e.activation(GF[0:1, 0:1], PG[0:1, 6:7], AF.Sigmoid,
                                         bias=BIAS["gcb3"][0:1, :]),
             reads=[PG.b, BIAS["gcb3"].b], writes=[GF.b])
        P.op("v", lambda e: e.tensor_scalar(GF[0:1, 1:2], GF[0:1, 0:1], 0.7, None, ALU.mult),
             reads=[GF.b], writes=[GF.b])
        P.barrier()
        esG.close()
        B.tap("gf", GF, [1, 2], F32)
        if B.cfg.get("l0_upto") == "ffngf":
            return

        twist_pairs = [(0, 4, 1.0), (1, 5, 1.0), (2, 6, -1.0), (3, 7, -1.0)]

        ncyc = {"ffncyc1": 1, "ffncyc2": 2}.get(B.cfg.get("l0_upto"), 3)
        for cyc in range(ncyc):
            for tc in range(NTC):
                tsl = slice(tc * TC, (tc + 1) * TC)
                for i in range(NFC):
                    P.op("v", lambda e, i=i, tsl=tsl: e.tensor_copy(O16[i][:, :], OUT32[i][:, tsl]),
                         reads=[OB[(i, tc)]], writes=[O16[i].b])
                for fo in range(NFC):
                    ph = PH[fo % 2]

                    def mk1(e, fo=fo, ph=ph):
                        last = None
                        for ec in range(NFC):
                            last = e.matmul(ph[:, :], CW1[ec][:, fo * 128:(fo + 1) * 128],
                                            O16[ec][:, :], start=(ec == 0), stop=(ec == NFC - 1))
                        return last
                    P.op("t", mk1, reads=[w.b for w in CW1] + [O16[ec].b for ec in range(NFC)], writes=[ph.b])
                    P.op("c", lambda e, fo=fo, ph=ph: e.activation(H1[fo][:, :], ph[:, :], AF.Gelu,
                                                                   bias=BIAS["cnb1"][:, fo:fo + 1]),
                         reads=[ph.b, BIAS["cnb1"].b], writes=[H1[fo].b])
                for fo in range(4):
                    ph = PH[fo % 2]

                    def mk2(e, fo=fo, ph=ph):
                        last = None
                        for ec in range(NFC):
                            last = e.matmul(ph[:, :], CW2[ec][:, fo * 128:(fo + 1) * 128],
                                            H1[ec][:, :], start=(ec == 0), stop=(ec == NFC - 1))
                        return last
                    P.op("t", mk2, reads=[w.b for w in CW2] + [h.b for h in H1], writes=[ph.b])
                    P.op("c", lambda e, fo=fo, ph=ph: e.activation(H2[fo][:, :], ph[:, :], AF.Gelu,
                                                                   bias=BIAS["cnb2"][:, fo:fo + 1]),
                         reads=[ph.b, BIAS["cnb2"].b], writes=[H2[fo].b])
                for fo in range(2):
                    ph = PH[fo % 2]

                    def mk3(e, fo=fo, ph=ph):
                        last = None
                        for ec in range(4):
                            last = e.matmul(ph[:, :], CW3[ec][:, fo * 128:(fo + 1) * 128],
                                            H2[ec][:, :], start=(ec == 0), stop=(ec == 3))
                        return last
                    P.op("t", mk3, reads=[w.b for w in CW3] + [h.b for h in H2], writes=[ph.b])
                    P.op("c", lambda e, fo=fo, ph=ph: e.activation(H3[fo][:, :], ph[:, :], AF.Gelu,
                                                                   bias=BIAS["cnb3"][:, fo:fo + 1]),
                         reads=[ph.b, BIAS["cnb3"].b], writes=[H3[fo].b])

                def mk4(e):
                    last = None
                    for ec in range(2):
                        last = e.matmul(PR2[0:1, :], CW4[ec][:, 0:1], H3[ec][:, :],
                                        start=(ec == 0), stop=(ec == 1))
                    return last
                P.op("t", mk4, reads=[w.b for w in CW4] + [h.b for h in H3], writes=[PR2.b])
                P.op("c", lambda e: e.activation(TF[0:1, :], PR2[0:1, :], AF.Sigmoid,
                                                 bias=BIAS["cnb4"][0:1, :]),
                     reads=[PR2.b, BIAS["cnb4"].b], writes=[TF.b])
                P.op("v", lambda e: e.tensor_scalar(TF[0:1, :], TF[0:1, :], 0.3, GF[0:1, 1:2], ALU.mult, ALU.add),
                     reads=[TF.b, GF.b], writes=[TF.b])
                P.op("v", lambda e: e.tensor_scalar(TF[0:1, :], TF[0:1, :], CC[0:1, 0:1], CC[0:1, 1:2],
                                                    ALU.mult, ALU.add),
                     reads=[TF.b, CC.b], writes=[TF.b])
                PCP = PCP2[tc % 2]
                bcast_hl(B, HBf, LBf, DFf, TF[0:1, :], PCP[:, :], 128, [TF.b], PCP.b)
                tw0 = TW[2 * (tc % 2)]
                tw1 = TW[2 * (tc % 2) + 1]
                for (a, bb, sgn) in twist_pairs:
                    P.op("v", lambda e, bb=bb, tsl=tsl, tw0=tw0, PCP=PCP: e.tensor_tensor(
                            tw0[:, :], OUT32[bb][:, tsl], PCP[:, :], ALU.mult),
                         reads=[OB[(bb, tc)], PCP.b], writes=[tw0.b])
                    P.op("v", lambda e, a=a, tsl=tsl, tw1=tw1, PCP=PCP: e.tensor_tensor(
                            tw1[:, :], OUT32[a][:, tsl], PCP[:, :], ALU.mult),
                         reads=[OB[(a, tc)], PCP.b], writes=[tw1.b])
                    op_a = ALU.add if sgn > 0 else ALU.subtract
                    op_b = ALU.subtract if sgn > 0 else ALU.add
                    P.op("v", lambda e, a=a, tsl=tsl, op_a=op_a, tw0=tw0: e.tensor_tensor(
                            OUT32[a][:, tsl], OUT32[a][:, tsl], tw0[:, :], op_a),
                         reads=[tw0.b, OB[(a, tc)]], writes=[OB[(a, tc)]])
                    P.op("v", lambda e, bb=bb, tsl=tsl, op_b=op_b, tw1=tw1: e.tensor_tensor(
                            OUT32[bb][:, tsl], OUT32[bb][:, tsl], tw1[:, :], op_b),
                         reads=[tw1.b, OB[(bb, tc)]], writes=[OB[(bb, tc)]])
        for i in range(NFC):
            P.op("v", lambda e, i=i: e.tensor_tensor(X32[i][:, :], X32[i][:, :], OUT32[i][:, :], ALU.add),
                 reads=[OB[(i, tc)] for tc in range(NTC)] + [X32[i].b], writes=[X32[i].b])



def build_layer1(B, sems):
    nc, P = B.nc, B.P
    X32 = B.consts["X32"]
    ONESR = B.consts["ONESR"]
    with contextlib.ExitStack() as esL:
        esV = contextlib.ExitStack()
        V = [B.sbuf(esV, f"V1_{st}", [128, 16 * 65], BF16) for st in range(NST)]
        esX = contextlib.ExitStack()
        XN16 = [B.sbuf(esX, f"XN16b_{i}", [128, T], BF16) for i in range(NFC)]
        GR8 = B.sbuf(esX, "GR8", [8, T], F32)
        esMR = contextlib.ExitStack()
        with contextlib.ExitStack() as es:
            M, R = layernorm_rows(B, esMR, es, "1")
            P.barrier()
        P.flush(nc, sems)

        with contextlib.ExitStack() as es:
            GW32 = B.sbuf(es, "GW32", [128, 64], F32)
            src = B.dram["gwT"].rearrange("(ec p) c -> p ec c", p=128)
            P.op("s", lambda e, src=src: e.dma_start(
                    out=GW32.ap.rearrange("p (ec c) -> p ec c", ec=8), in_=src),
                 reads=[B.bufs["gwT"]], writes=[GW32.b])
            IOT = B.sbuf(es, "IOT", [128, 8], F32)
            P.op("s", lambda e: e.dma_start(out=IOT[:, :], in_=B.dram["IOTA8"][:, :]),
                 reads=[B.bufs["IOTA8"]], writes=[IOT.b])
            IDM = B.sbuf(es, "IDM", [128, 128], F32)
            P.op("s", lambda e: e.dma_start(out=IDM[:, :], in_=B.dram["ID128"][:, :]),
                 reads=[B.bufs["ID128"]], writes=[IDM.b])
            GL = B.sbuf(es, "GL", [128, 8], F32)
            MX = B.sbuf(es, "MX", [128, 8], F32)
            MI = B.sbuf(es, "MI", [128, 8], U32)
            MIF = B.sbuf(es, "MIF", [128, 2], F32)
            GSC = B.sbuf(es, "GSC", [128, 4], F32)
            OH = [B.sbuf(es, f"OH{i}", [128, 8], F32) for i in range(2)]
            GT = B.sbuf(es, "GT", [128, 8], F32)
            G4 = B.psum(es, "G4", [128, 32], F32)
            PT8 = B.psum(es, "PT8", [8, 128], F32)

            def gate_cb(tc, XA):
                def mkgl(e, tc=tc):
                    last = None
                    for j in range(4):
                        for ec in range(NFC):
                            last = e.matmul(G4[:, j * 8:(j + 1) * 8],
                                            XA[ec][:, j * 128:(j + 1) * 128],
                                            GW32[:, ec * 8:(ec + 1) * 8],
                                            start=(ec == 0), stop=(ec == NFC - 1),
                                            skip_group_check=True)
                    return last
                P.op("t", mkgl, reads=[xa.b for xa in XA] + [GW32.b], writes=[G4.b])
                for j in range(4):
                    tt = tc * 4 + j
                    P.op("v", lambda e, j=j: e.tensor_copy(GL[:, :], G4[:, j * 8:(j + 1) * 8]),
                         reads=[G4.b], writes=[GL.b])
                    P.op("v", lambda e: e.max(MX[:, :], GL[:, :]), reads=[GL.b], writes=[MX.b])
                    P.op("v", lambda e: e.max_index(MI[:, :], MX[:, :], GL[:, :]),
                         reads=[MX.b, GL.b], writes=[MI.b])
                    P.op("v", lambda e: e.tensor_copy(MIF[:, :], MI[:, 0:2]), reads=[MI.b], writes=[MIF.b])
                    P.op("v", lambda e: e.tensor_sub(GSC[:, 0:1], MX[:, 0:1], MX[:, 1:2]),
                         reads=[MX.b], writes=[GSC.b])
                    P.op("c", lambda e: e.activation(GSC[:, 2:3], GSC[:, 0:1], AF.Sigmoid),
                         reads=[GSC.b], writes=[GSC.b])
                    P.op("v", lambda e: e.tensor_scalar(GSC[:, 3:4], GSC[:, 2:3], -1.0, 1.0, ALU.mult, ALU.add),
                         reads=[GSC.b], writes=[GSC.b])
                    P.op("v", lambda e: e.tensor_scalar(OH[0][:, :], IOT[:, :], MIF[:, 0:1], None, ALU.is_equal),
                         reads=[IOT.b, MIF.b], writes=[OH[0].b])
                    P.op("v", lambda e: e.tensor_scalar(OH[1][:, :], IOT[:, :], MIF[:, 1:2], None, ALU.is_equal),
                         reads=[IOT.b, MIF.b], writes=[OH[1].b])
                    P.op("v", lambda e: e.tensor_scalar(OH[0][:, :], OH[0][:, :], GSC[:, 2:3], None, ALU.mult),
                         reads=[OH[0].b, GSC.b], writes=[OH[0].b])
                    P.op("v", lambda e: e.tensor_scalar(GT[:, :], OH[1][:, :], GSC[:, 3:4], None, ALU.mult),
                         reads=[OH[1].b, GSC.b], writes=[GT.b])
                    P.op("v", lambda e: e.tensor_add(GT[:, :], GT[:, :], OH[0][:, :]),
                         reads=[GT.b, OH[0].b], writes=[GT.b])
                    P.op("t", lambda e: e.transpose(PT8[0:8, :], GT[:, :], IDM[:, :]),
                         reads=[GT.b, IDM.b], writes=[PT8.b])
                    P.op("v", lambda e, tt=tt: e.tensor_copy(GR8[0:8, tt * 128:(tt + 1) * 128], PT8[0:8, :]),
                         reads=[PT8.b], writes=[GR8.b])

            ln_apply_bf16(B, es, M, R, XN16, gate_cb=gate_cb)
            P.op("s", lambda e: e.dma_start(out=B.dram["grows"][:, :], in_=GR8[0:8, :]),
                 reads=[GR8.b], writes=[B.bufs["grows"]])
            P.barrier()
        P.flush(nc, sems)
        esMR.close()
        if B.cfg.get("l1_upto") == "ln":
            esMR.close() if False else None
            esX.close()
            esV.close()
            return

        for st in range(NST):
            dstap = V[st].ap.rearrange("p (g c) -> p g c", g=16)[:, :, 64:65]
            P.op("v", lambda e, dstap=dstap: e.memset(dstap, 1.0), writes=[V[st].b])
        with contextlib.ExitStack() as es:
            BQ = B.sbuf(es, "BQs", [128, 8], F32)
            P.op("s", lambda e: e.dma_start(out=BQ[:, :], in_=B.dram["bq"][:, :]),
                 reads=[B.bufs["bq"]], writes=[BQ.b])
            BK = B.sbuf(es, "BKs", [128, 8], F32)
            P.op("s", lambda e: e.dma_start(out=BK[:, :], in_=B.dram["bk"][:, :]),
                 reads=[B.bufs["bk"]], writes=[BK.b])
            BVR = B.sbuf(es, "BVR", [1, D], F32)
            P.op("s", lambda e: e.dma_start(out=BVR[:, :], in_=B.dram["bv"][:, :]),
                 reads=[B.bufs["bv"]], writes=[BVR.b])
            BVB = B.sbuf(es, "BVB", [128, D], F32)
            HBv = B.sbuf(es, "HBv", [1, TC], BF16)
            LBv = B.sbuf(es, "LBv", [1, TC], BF16)
            DFv = B.sbuf(es, "DFv", [1, TC], F32)
            with contextlib.ExitStack() as ebv:
                PBV = B.psum(ebv, "PBV", [128, TC], F32)
                for hh in range(2):
                    bcast_hl(B, HBv, LBv, DFv, BVR[0:1, hh * 512:(hh + 1) * 512], PBV[:, :], 128,
                             [BVR.b], PBV.b)
                    P.op("v", lambda e, hh=hh: e.tensor_copy(BVB[:, hh * 512:(hh + 1) * 512], PBV[:, :]),
                         reads=[PBV.b], writes=[BVB.b])
                P.barrier()
            proj_qkv(B, es, [("L1Wq", "qTs", BQ), ("L1Wk", "kTs", BK)],
                     XN16, V=V, vspec=("L1Wv", BVB))
            P.barrier()
        P.flush(nc, sems)
        for i in range(NFC):
            P.op("s", lambda e, i=i: e.dma_start(out=B.dram["xn16s"][i * 128:(i + 1) * 128, :],
                                                 in_=XN16[i][:, :]),
                 reads=[XN16[i].b], writes=[B.bufs["xn16s"]])
        P.barrier()
        P.flush(nc, sems)
        esX.close()
        if B.cfg.get("l1_upto") == "qkv":
            esV.close()
            return

        attn_l1(B, esL, sems, V)
        P.barrier()
        P.flush(nc, sems)
        esV.close()
        if B.cfg.get("l1_upto") == "attn":
            return

        X32 = B.consts["X32"]
        for i in range(NFC):
            P.op("s", lambda e, i=i: e.dma_start(out=B.dram["x32s"][i * 128:(i + 1) * 128, :],
                                                 in_=X32[i][:, :]),
                 reads=[X32[i].b], writes=[B.bufs["x32s"]])
        P.barrier()
        P.flush(nc, sems)
        B.esX32.close()
        B.x32_parked = True

        moe_l1(B, esL, sems, XN16)
        P.barrier()
        P.flush(nc, sems)


def attn_l1(B, esL, sems, V):
    nc, P = B.nc, B.P
    X32 = B.consts["X32"]
    ONESC = B.consts["ONESC"]
    ONESR = B.consts["ONESR"]
    ONESR16 = B.consts["ONESR16"]
    with contextlib.ExitStack() as es:
        AO = [B.sbuf(es, f"AO1_{i}", [128, T], BF16) for i in range(NFC)]
        with contextlib.ExitStack() as ep:
            ALBs = B.sbuf(ep, "ALBs", [128, 256], F32)
            P.op("s", lambda e: e.dma_start(out=ALBs[:, :], in_=B.dram["ALB"][:, :]),
                 reads=[B.bufs["ALB"]], writes=[ALBs.b])
            CMs = B.sbuf(ep, "CMs", [128, 4 * TC], BF16)
            P.op("s", lambda e: e.dma_start(out=CMs[:, :], in_=B.dram["CM"][:, :]),
                 reads=[B.bufs["CM"]], writes=[CMs.b])
            QP = [B.sbuf(ep, f"QP1_{i}", [65, T], BF16) for i in range(1)]
            KP = [B.sbuf(ep, f"KP1_{i}", [65, T], BF16) for i in range(1)]
            P.op("v", lambda e: e.memset(KP[0][64:65, :], 1.0), writes=[KP[0].b])
            EB = [B.sbuf(ep, f"EB{i}", [128, 1024], BF16) for i in range(2)]
            ATTS = [B.sbuf(ep, f"ATTS1_{i}", [64, TC], BF16) for i in range(1)]
            RAWH = B.sbuf(ep, "RAWH", [64, T], F32)
            ZHR = B.sbuf(ep, "ZHR", [1, T], F32)
            SQ16 = B.sbuf(ep, "SQ161", [64, TC], BF16)
            ONESC16 = B.sbuf(ep, "ONESC16", [128, 1], BF16)
            P.op("v", lambda e: e.memset(ONESC16[:, :], 1.0), writes=[ONESC16.b])
            HB = B.sbuf(ep, "HBr1", [1, TC], BF16)
            LB = B.sbuf(ep, "LBr1", [1, TC], BF16)
            DF = B.sbuf(ep, "DFr1", [1, TC], F32)
            NRM = B.sbuf(ep, "NRM1", [1, 4], F32)
            DROW = B.sbuf(ep, "DROW", [1, TC], F32)
            D16 = B.sbuf(ep, "D16", [1, T], BF16)
            PS2 = [B.psum(ep, f"PS1_{i}", [128, 1024], F32) for i in range(2)]
            PA = [B.psum(ep, f"PA1h_{i}", [65, TC], F32) for i in range(2)]
            PRW = B.psum(ep, "PRW1", [1, TC], F32)
            PBC = B.psum(ep, "PBC1", [128, TC], F32)

            def emit_combine1(h):
                P.op("c", lambda e: e.activation(ZHR[0:1, :], ZHR[0:1, :], AF.Ln), reads=[ZHR.b], writes=[ZHR.b])
                P.op("c", lambda e: e.activation(ZHR[0:1, :], ZHR[0:1, :], AF.Exp, scale=-1.0),
                     reads=[ZHR.b], writes=[ZHR.b])
                ftile = h // 2
                rbase = 64 * (h % 2)
                for tc in range(NTC):
                    tsl = slice(tc * TC, (tc + 1) * TC)
                    bcast_hl(B, HB, LB, DF, ZHR[0:1, tsl], PBC[0:64, :], 64, [ZHR.b], PBC.b)
                    if rbase == 0:
                        P.op("v", lambda e, ftile=ftile, tsl=tsl: e.tensor_tensor(
                                AO[ftile][0:64, tsl], RAWH[:, tsl], PBC[0:64, :], ALU.mult),
                             reads=[RAWH.b, PBC.b], writes=[AO[ftile].b])
                    else:
                        ats = ATTS[0]
                        P.op("v", lambda e, ats=ats, tsl=tsl: e.tensor_tensor(
                                ats[:, :], RAWH[:, tsl], PBC[0:64, :], ALU.mult),
                             reads=[RAWH.b, PBC.b], writes=[ats.b])
                        P.op("g", lambda e, ftile=ftile, tsl=tsl, ats=ats: e.dma_start(
                                out=AO[ftile][64:128, tsl], in_=ats[:, :]),
                             reads=[ats.b], writes=[AO[ftile].b])

            for h in range(16):
                qp, kp = QP[0], KP[0]
                P.op("g", lambda e, h=h, qp=qp: e.dma_start(out=qp[0:64, :], in_=B.dram["qTs"][h * 64:(h + 1) * 64, :]),
                     reads=[B.bufs["qTs"]], writes=[qp.b])
                P.op("g", lambda e, h=h, kp=kp: e.dma_start(out=kp[0:64, :], in_=B.dram["kTs"][h * 64:(h + 1) * 64, :]),
                     reads=[B.bufs["kTs"]], writes=[kp.b])
                for src, col in ((qp, 0), (kp, 1)):
                    for tc in range(NTC):
                        P.op("c", lambda e, src=src, tc=tc: e.activation(
                                SQ16[:, :], src[0:64, tc * TC:(tc + 1) * TC], AF.Square),
                             reads=[src.b], writes=[SQ16.b])
                        ps_ = PS2[tc // 2]
                        P.op("t", lambda e, tc=tc, ps_=ps_: e.matmul(
                                ps_[0:1, (tc % 2) * TC:((tc % 2) + 1) * TC],
                                ONESC16[0:64, :], SQ16[:, :],
                                start=True, stop=True, skip_group_check=True),
                             reads=[SQ16.b, ONESC16.b], writes=[ps_.b])
                    P.op("v", lambda e: e.reduce_max(NRM[0:1, 2:3], PS2[0][0:1, :], axis=AX.X),
                         reads=[PS2[0].b], writes=[NRM.b])
                    P.op("v", lambda e: e.reduce_max(NRM[0:1, 3:4], PS2[1][0:1, :], axis=AX.X),
                         reads=[PS2[1].b], writes=[NRM.b])
                    P.op("v", lambda e, col=col: e.tensor_max(NRM[0:1, col:col + 1],
                                                              NRM[0:1, 2:3], NRM[0:1, 3:4]),
                         reads=[NRM.b], writes=[NRM.b])
                P.op("v", lambda e: e.tensor_tensor(NRM[0:1, 2:3], NRM[0:1, 0:1], NRM[0:1, 1:2], ALU.mult),
                     reads=[NRM.b], writes=[NRM.b])
                P.op("c", lambda e: e.activation(NRM[0:1, 2:3], NRM[0:1, 2:3], AF.Ln),
                     reads=[NRM.b], writes=[NRM.b])
                P.op("c", lambda e: e.activation(NRM[0:1, 2:3], NRM[0:1, 2:3], AF.Exp, scale=0.5),
                     reads=[NRM.b], writes=[NRM.b])
                for hf in range(4):
                    P.op("s", lambda e, h=h, hf=hf: e.dma_start(
                            out=DROW[0:1, :], in_=B.dram["A8"][h:h + 1, hf * TC:(hf + 1) * TC]),
                         reads=[B.bufs["A8"]], writes=[DROW.b])
                    P.op("v", lambda e: e.tensor_scalar(DROW[0:1, :], DROW[0:1, :], NRM[0:1, 2:3],
                                                        None, ALU.subtract),
                         reads=[DROW.b, NRM.b], writes=[DROW.b])
                    P.op("v", lambda e, hf=hf, qp=qp: e.tensor_copy(qp[64:65, hf * TC:(hf + 1) * TC],
                                                                    DROW[0:1, :]),
                         reads=[DROW.b], writes=[qp.b])
                if h > 0:
                    emit_combine1(h - 1)

                for tcp in range(2):
                    tcA = 2 * tcp
                    tcB = 2 * tcp + 1
                    tAsl = slice(tcA * TC, (tcA + 1) * TC)
                    tBsl = slice(tcB * TC, (tcB + 1) * TC)
                    nsi = 4 * tcB + 4

                    def emit_av1(si, h=h, tcA=tcA, tcB=tcB, nsi=nsi):
                        eb = EB[si % 2]

                        def mkav(e, si=si, h=h, eb=eb):
                            last = None
                            for q, tcx in ((0, tcA), (1, tcB)):
                                if si > 4 * tcx + 3:
                                    continue
                                last = e.matmul(PA[q][0:65, :],
                                                V[si][:, h * 65:(h + 1) * 65],
                                                eb[:, q * TC:(q + 1) * TC],
                                                start=(si == 0), stop=(si == min(4 * tcx + 3, nsi - 1)),
                                                skip_group_check=True)
                            return last
                        P.op("t", mkav, reads=[eb.b, V[si].b], writes=[PA[0].b, PA[1].b])

                    for si in range(nsi):
                        ps_ = PS2[si % 2]
                        eb = EB[si % 2]
                        do_A = si <= 4 * tcA + 3

                        def mksc(e, si=si, kp=kp, qp=qp, tAsl=tAsl, tBsl=tBsl, ps_=ps_, do_A=do_A):
                            last = None
                            for q, tsl, act in ((0, tAsl, do_A), (1, tBsl, True)):
                                if not act:
                                    continue
                                last = e.matmul(ps_[:, q * TC:(q + 1) * TC],
                                                kp[0:65, si * 128:(si + 1) * 128],
                                                qp[0:65, tsl], start=True, stop=True,
                                                skip_group_check=True)
                            return last
                        P.op("t", mksc, reads=[kp.b, qp.b], writes=[ps_.b])
                        for q, tcx, act in ((0, tcA, do_A), (1, tcB, True)):
                            if not act:
                                continue
                            if 4 * tcx <= si <= 4 * tcx + 3:
                                mj = si - 4 * tcx
                                P.op("v", lambda e, q=q, mj=mj, ps_=ps_: e.tensor_tensor(
                                        ps_[:, q * TC:(q + 1) * TC], ps_[:, q * TC:(q + 1) * TC],
                                        CMs[:, mj * TC:(mj + 1) * TC], ALU.add),
                                     reads=[CMs.b], writes=[ps_.b])
                            P.op("c", lambda e, q=q, si=si, h=h, eb=eb, ps_=ps_: e.activation(
                                    eb[:, q * TC:(q + 1) * TC], ps_[:, q * TC:(q + 1) * TC], AF.Exp,
                                    bias=ALBs[:, h * 16 + si:h * 16 + si + 1], scale=SC1),
                                 reads=[ps_.b, ALBs.b], writes=[eb.b])
                        if si > 0:
                            emit_av1(si - 1)
                    emit_av1(nsi - 1)
                    for q, tsl in ((0, tAsl), (1, tBsl)):
                        P.op("v", lambda e, tsl=tsl, q=q: e.tensor_copy(RAWH[:, tsl], PA[q][0:64, :]),
                             reads=[PA[q].b], writes=[RAWH.b])
                        P.op("v", lambda e, tsl=tsl, q=q: e.tensor_copy(ZHR[0:1, tsl], PA[q][64:65, :]),
                             reads=[PA[q].b], writes=[ZHR.b])
            emit_combine1(15)
            P.barrier()

        with contextlib.ExitStack() as ew:
            BO = B.sbuf(ew, "BOs", [128, 8], F32)
            P.op("s", lambda e: e.dma_start(out=BO[:, :], in_=B.dram["bo"][:, :]),
                 reads=[B.bufs["bo"]], writes=[BO.b])
            W = load_w_bf16(B, ew, "L1Wo", D, D, "wo1")
            PW = [B.psum(ew, f"PWo1_{i}", [128, 2048], F32) for i in range(2)]
            for fo in range(NFC):
                pw = PW[fo % 2]

                def mkwo(e, fo=fo, pw=pw):
                    last = None
                    for ec in range(NFC):
                        for tc in range(NTC):
                            last = e.matmul(pw[:, tc * TC:(tc + 1) * TC],
                                            W[ec][:, fo * 128:(fo + 1) * 128],
                                            AO[ec][:, tc * TC:(tc + 1) * TC],
                                            start=(ec == 0), stop=(ec == NFC - 1),
                                            skip_group_check=True)
                    return last
                P.op("t", mkwo, reads=[W[ec].b for ec in range(NFC)] + [AO[ec].b for ec in range(NFC)],
                     writes=[pw.b])
                for tc in range(NTC):
                    tsl = slice(tc * TC, (tc + 1) * TC)
                    P.op("v", lambda e, fo=fo, tsl=tsl, pw=pw, tc=tc: e.scalar_tensor_tensor(
                            X32[fo][:, tsl], pw[:, tc * TC:(tc + 1) * TC], BO[:, fo:fo + 1],
                            X32[fo][:, tsl], ALU.add, ALU.add),
                         reads=[pw.b, X32[fo].b, BO.b], writes=[X32[fo].b])
            P.barrier()

def moe_l1(B, esL, sems, XN16_unused):
    nc, P = B.nc, B.P
    ONESR = B.consts["ONESR"]
    with contextlib.ExitStack() as es:
        XN16 = [B.sbuf(es, f"XN16m_{i}", [128, T], BF16) for i in range(NFC)]
        for i in range(NFC):
            P.op("s", lambda e, i=i: e.dma_start(out=XN16[i][:, :],
                                                 in_=B.dram["xn16s"][i * 128:(i + 1) * 128, :]),
                 reads=[B.bufs["xn16s"]], writes=[XN16[i].b])
        HT = [B.sbuf(es, f"HT_{i}", [128, T], BF16) for i in range(32)]
        W1S = [B.sbuf(es, f"W1S_{i}", [128, 1024], BF16) for i in range(2)]
        W2S = [B.sbuf(es, f"W2S_{i}", [128, 4096], BF16) for i in range(2)]
        BCG = B.sbuf(es, "BCG", [128, T], F32)
        TMP = [B.sbuf(es, f"TMPm{i}", [128, T], F32) for i in range(1)]
        GRo = B.sbuf(es, "GRo", [1, 1024], F32)
        HBm = B.sbuf(es, "HBm", [1, TC], BF16)
        LBm = B.sbuf(es, "LBm", [1, TC], BF16)
        DFm = B.sbuf(es, "DFm", [1, TC], F32)
        PH2 = [B.psum(es, f"PH2_{i}", [128, T], F32) for i in range(2)]

        for ei in range(9):
            if ei == 0:
                w1name, w2name = "sw1c", "sw2c"
            else:
                w1name, w2name = f"ew1c{ei - 1}", f"ew2c{ei - 1}"
            if ei > 0:
                for hf in range(2):
                    P.op("s", lambda e, ei=ei, hf=hf: e.dma_start(
                            out=GRo[0:1, :], in_=B.dram["grows"][ei - 1:ei, hf * 1024:(hf + 1) * 1024]),
                         reads=[B.bufs["grows"]], writes=[GRo.b])
                    for q in range(2):
                        tsl = slice((2 * hf + q) * TC, (2 * hf + q + 1) * TC)
                        bcast_hl(B, HBm, LBm, DFm, GRo[0:1, q * TC:(q + 1) * TC], PH2[0][:, 0:TC], 128,
                                 [GRo.b], PH2[0].b)
                        P.op("v", lambda e, tsl=tsl: e.tensor_copy(BCG[:, tsl], PH2[0][:, 0:TC]),
                             reads=[PH2[0].b], writes=[BCG.b])
            w1src = B.dram[w1name]
            for hc in range(32):
                w1s = W1S[hc % 2]
                ph = PH2[hc % 2]
                P.op("s", lambda e, hc=hc, w1s=w1s, w1src=w1src: e.dma_start(
                        out=w1s[:, :],
                        in_=w1src[hc * 128:(hc + 1) * 128, :]),
                     reads=[B.bufs[w1name]], writes=[w1s.b])

                def mkh(e, hc=hc, w1s=w1s, ph=ph):
                    last = None
                    for ec in range(NFC):
                        for tc in range(NTC):
                            last = e.matmul(ph[:, tc * TC:(tc + 1) * TC],
                                            w1s[:, ec * 128:(ec + 1) * 128],
                                            XN16[ec][:, tc * TC:(tc + 1) * TC],
                                            start=(ec == 0), stop=(ec == NFC - 1),
                                            skip_group_check=True)
                    return last
                P.op("t", mkh, reads=[w1s.b] + [XN16[ec].b for ec in range(NFC)], writes=[ph.b])
                P.op("c", lambda e, hc=hc, ph=ph: e.activation(HT[hc][:, :], ph[:, :], AF.Silu),
                     reads=[ph.b], writes=[HT[hc].b])
            w2src = B.dram[w2name]
            for fo in range(NFC):
                w2s = W2S[fo % 2]
                py = PH2[fo % 2]
                P.op("s", lambda e, fo=fo, w2s=w2s, w2src=w2src: e.dma_start(
                        out=w2s[:, :],
                        in_=w2src[fo * 128:(fo + 1) * 128, :]),
                     reads=[B.bufs[w2name]], writes=[w2s.b])

                def mky(e, fo=fo, w2s=w2s, py=py):
                    last = None
                    for hc in range(32):
                        for tc in range(NTC):
                            last = e.matmul(py[:, tc * TC:(tc + 1) * TC],
                                            w2s[:, hc * 128:(hc + 1) * 128],
                                            HT[hc][:, tc * TC:(tc + 1) * TC],
                                            start=(hc == 0), stop=(hc == 31),
                                            skip_group_check=True)
                    return last
                P.op("t", mky, reads=[w2s.b] + [h.b for h in HT], writes=[py.b])
                tmp = TMP[0]
                if ei == 0:
                    P.op("v", lambda e, tmp=tmp, py=py: e.tensor_copy(tmp[:, :], py[:, :]),
                         reads=[py.b], writes=[tmp.b])
                    P.op("g", lambda e, fo=fo, tmp=tmp: e.dma_start(
                            out=B.dram["y32s"][fo * 128:(fo + 1) * 128, :], in_=tmp[:, :]),
                         reads=[tmp.b], writes=[B.bufs["y32s"]])
                else:
                    P.op("v", lambda e, tmp=tmp, py=py: e.tensor_tensor(tmp[:, :], py[:, :], BCG[:, :],
                                                                        ALU.mult),
                         reads=[py.b, BCG.b], writes=[tmp.b])
                    P.op("g", lambda e, fo=fo, tmp=tmp: e.dma_start(
                            out=B.dram["y32s"][fo * 128:(fo + 1) * 128, :], in_=tmp[:, :],
                            accum_op=ALU.add),
                         reads=[tmp.b], writes=[B.bufs["y32s"]])



def _host_arrays(inputs, layers):
    f = np.float32
    bf = ml_dtypes.bfloat16
    A = {}
    if 0 in layers:
        A["L0WqT"] = np.ascontiguousarray(np.asarray(inputs["L0_attn_Wq"]).T.astype(f).astype(bf))
        A["L0WkT"] = np.ascontiguousarray(np.asarray(inputs["L0_attn_Wk"]).T.astype(f).astype(bf))
        A["L0WvT"] = np.ascontiguousarray(np.asarray(inputs["L0_attn_Wv"]).T.astype(f).astype(bf))
        A["L0WoT"] = np.ascontiguousarray(np.asarray(inputs["L0_attn_Wo"]).T.astype(f).astype(bf))
        A["cnw1"] = np.ascontiguousarray(np.asarray(inputs["L0_ffn_cn_w1"]).astype(f).astype(bf))
        A["cnw2"] = np.ascontiguousarray(np.asarray(inputs["L0_ffn_cn_w2"]).astype(f).astype(bf))
        A["cnw3"] = np.ascontiguousarray(np.asarray(inputs["L0_ffn_cn_w3"]).astype(f).astype(bf))
        A["cnw4"] = np.ascontiguousarray(np.asarray(inputs["L0_ffn_cn_w4"]).astype(f).astype(bf))
        A["gcw1"] = np.ascontiguousarray(np.asarray(inputs["L0_ffn_gc_w1"]).astype(f).astype(bf))
        A["gcw2"] = np.ascontiguousarray(np.asarray(inputs["L0_ffn_gc_w2"]).astype(f).astype(bf))
        A["gcw3"] = np.ascontiguousarray(np.asarray(inputs["L0_ffn_gc_w3"]).astype(f).astype(bf))
        A["cnb1"] = np.ascontiguousarray(np.asarray(inputs["L0_ffn_cn_b1"]).astype(f).reshape(8, 128).T)
        A["cnb2"] = np.ascontiguousarray(np.asarray(inputs["L0_ffn_cn_b2"]).astype(f).reshape(4, 128).T)
        A["cnb3"] = np.ascontiguousarray(np.asarray(inputs["L0_ffn_cn_b3"]).astype(f).reshape(2, 128).T)
        A["cnb4"] = np.asarray(inputs["L0_ffn_cn_b4"]).astype(f).reshape(1, 1)
        A["gcb1"] = np.ascontiguousarray(np.asarray(inputs["L0_ffn_gc_b1"]).astype(f).reshape(4, 128).T)
        A["gcb2"] = np.ascontiguousarray(np.asarray(inputs["L0_ffn_gc_b2"]).astype(f).reshape(2, 128).T)
        A["gcb3"] = np.asarray(inputs["L0_ffn_gc_b3"]).astype(f).reshape(1, 1)
        A["arange"] = np.asarray(inputs["L0_ffn_adaptive_range"]).astype(f).reshape(1, 1)
        A["lamq1"] = np.asarray(inputs["L0_attn_lambda_q1"]).astype(f).reshape(1, 32)
        A["lamk1"] = np.asarray(inputs["L0_attn_lambda_k1"]).astype(f).reshape(1, 32)
        A["lamq2"] = np.asarray(inputs["L0_attn_lambda_q2"]).astype(f).reshape(1, 32)
        A["lamk2"] = np.asarray(inputs["L0_attn_lambda_k2"]).astype(f).reshape(1, 32)
        A["subln"] = np.asarray(inputs["L0_attn_subln_w"]).astype(f).reshape(64, 1)
    if 1 in layers:
        A["L1Wq"] = np.ascontiguousarray(np.asarray(inputs["L1_attn_Wq"]).astype(f).astype(bf))
        A["L1Wk"] = np.ascontiguousarray(np.asarray(inputs["L1_attn_Wk"]).astype(f).astype(bf))
        A["L1Wv"] = np.ascontiguousarray(np.asarray(inputs["L1_attn_Wv"]).astype(f).astype(bf))
        A["L1Wo"] = np.ascontiguousarray(np.asarray(inputs["L1_attn_Wo"]).astype(f).astype(bf))
        A["bq"] = np.ascontiguousarray(np.asarray(inputs["L1_attn_bq"]).astype(f).reshape(8, 128).T)
        A["bk"] = np.ascontiguousarray(np.asarray(inputs["L1_attn_bk"]).astype(f).reshape(8, 128).T)
        A["bv"] = np.asarray(inputs["L1_attn_bv"]).astype(f).reshape(1, D)
        A["bo"] = np.ascontiguousarray(np.asarray(inputs["L1_attn_bo"]).astype(f).reshape(8, 128).T)
        def pack_w1(w):
            return np.ascontiguousarray(
                w.reshape(8, 128, 32, 128).transpose(2, 1, 0, 3).reshape(4096, 1024))

        def pack_w2(w):
            return np.ascontiguousarray(
                w.reshape(32, 128, 8, 128).transpose(2, 1, 0, 3).reshape(1024, 4096))

        A["sw1c"] = pack_w1(np.asarray(inputs["L1_ffn_shared_w1"]).T.astype(f).astype(bf))
        A["sw2c"] = pack_w2(np.asarray(inputs["L1_ffn_shared_w2"]).T.astype(f).astype(bf))
        for e in range(8):
            A[f"ew1c{e}"] = pack_w1(np.asarray(inputs["L1_ffn_experts_w1"])[e].T.astype(f).astype(bf))
            A[f"ew2c{e}"] = pack_w2(np.asarray(inputs["L1_ffn_experts_w2"])[e].T.astype(f).astype(bf))
        A["gwT"] = np.ascontiguousarray(np.asarray(inputs["L1_ffn_gate_w"]).T.astype(f))
        slopes = np.array(_get_slopes(16), dtype=np.float64)
        tpos = np.arange(T, dtype=np.float64)
        alibi = -slopes[:, None] * (T - 1 - tpos)[None, :]
        ALBc = np.zeros((128, 256), f)
        for h in range(16):
            for si in range(16):
                ALBc[:, h * 16 + si] = alibi[h, si * 128:(si + 1) * 128]
        A["ALB"] = ALBc
        A["A8"] = (-8.0 * alibi).astype(f)
        CMc = np.zeros((128, 4 * TC), f)
        ds = np.arange(128)[:, None]
        dt = np.arange(TC)[None, :]
        for j in range(4):
            CMc[:, j * TC:(j + 1) * TC] = np.where(dt >= ds + j * 128, 0.0, -2.0e38).astype(f)
        A["CM"] = CMc.astype(ml_dtypes.bfloat16)
        A["IOTA8"] = np.tile(np.arange(8, dtype=f)[None, :], (128, 1))
    A["ID128"] = np.eye(128, dtype=f)
    return A


def kernel(**inputs):
    cfg = dict(CFG)
    layers = tuple(cfg["layers"])
    n_cores = cfg["n_cores"]
    B = build_program(cfg)
    shared = _host_arrays(inputs, layers)
    x = np.asarray(inputs["L0_attn_query"], dtype=np.float32)
    in_maps = []
    for c in range(n_cores):
        m = dict(shared)
        m["xT"] = np.ascontiguousarray(x[c].T)
        in_maps.append(m)
    res = run_bass_kernel_spmd(B.nc, in_maps, core_ids=list(range(n_cores)))
    outs = np.stack([np.ascontiguousarray(res.results[c]["out"].T) for c in range(n_cores)])
    if cfg["debug"]:
        dbg = {nm: [res.results[c][nm] for c in range(n_cores)] for nm in B.debug_outs}
        return outs, dbg
    return outs
